# version 04
# baseline (speedup 1.0000x reference)
import numpy as np
import ml_dtypes
import concourse.bass as bass
import concourse.mybir as mybir
from concourse.bass_utils import run_bass_kernel_spmd

F32 = mybir.dt.float32
BF16 = mybir.dt.bfloat16
AF = mybir.ActivationFunctionType
ALU = mybir.AluOpType
AX = mybir.AxisListType

NCORES = 8
D = 2048
KC = 16
H = 16
DH = 128
R = 16
WIN = 512
TOK = 1024
HALO = 512
T = TOK + HALO
FF = 8192
FFC = 64
SQ = 512
EPS_LN = 1e-5
EPS_RMS = 1e-6
SCALE = float(1.0 / np.sqrt(DH))
NEG = -1.0e9

R_X = 0
R_XH = 65536
R_HV = 98304
R_H = 147456
R_WR = 180224
R_FS = 196608
R_SM = 207872
ARENA_BYTES = 209920 + 256

A_T1 = R_XH + 0
A_T2 = R_XH + 2048
A_SC = R_XH + 4096
A_SC2 = R_XH + 6656
A_KBF = R_XH + 9216
A_VF = R_XH + 12288
A_P = R_XH + 15360
A_P2 = R_XH + 16640
A_PT = R_XH + 17920
A_CS = R_XH + 18432
A_SN = R_XH + 21504
A_NM = R_XH + 24576
A_LB = R_XH + 24608
A_SC3 = R_XH + 27680
A_P3 = R_XH + 30240
A_NM3 = R_XH + 31520
O_OA = A_CS + 0
O_OB = A_CS + 1024
O_UO = A_CS + 2048

S_IDF = R_SM + 0
S_IDB = R_SM + 512
S_ON1 = R_SM + 768
S_ONB = R_SM + 1280
S_GAM = R_SM + 1312
S_DEN = R_SM + 1376
S_MH = R_SM + 1536
S_MD = R_SM + 1792
S_CP = R_SM + 2048

FS_MEAN = R_FS + 0
FS_RSTD = R_FS + 2048
FS_TMP = R_FS + 6144
FS_COLS = R_FS + 4096
FS_ROW0 = R_FS + 4224
FS_ROW1 = R_FS + 4736
FS_DUMP = R_FS + 5248
FS_SG = [R_FS + 0, R_FS + 2048]
SP_A = [R_FS + 4096, R_FS + 5120, R_FS + 6144, R_FS + 7168]
SP_W = [R_FS + 8192, R_FS + 9216, R_FS + 10240]
FS_UQ = R_FS + 0
FS_UK = R_FS + 3072
FS_UV = R_FS + 6144

L2_MEAN = R_H + 0
L2_RSTD = R_H + 6144
L2_TMP = R_H + 12288
ST0 = R_H + 0
ST1 = R_H + 8192
AT_COLS = R_H + 18432
AT_ROW0 = AT_COLS + 128
AT_ROW1 = AT_ROW0 + 512
AT_DUMP0 = AT_ROW1 + 512
AT_DUMP1 = AT_DUMP0 + 512
AT_QA = AT_DUMP1 + 512


class Sem:
    __slots__ = ("h", "n")

    def __init__(self, h):
        self.h = h
        self.n = 0


class Builder:
    def __init__(self, stages=("ffn1", "attn", "ffn2")):
        self.stages = stages
        nc = bass.Bass("TRN2", target_bir_lowering=False)
        self.nc = nc
        ap = lambda t: t.ap()
        self.d_x = ap(nc.dram_tensor("x", [T, D], F32, kind="ExternalInput"))
        self.d_y = ap(nc.dram_tensor("y", [TOK, D], F32, kind="ExternalOutput"))
        self.d_wg = ap(nc.dram_tensor("wg", [FFC, 128, 2048], F32, kind="ExternalInput"))
        self.d_wu = ap(nc.dram_tensor("wu", [FFC, 128, 2048], F32, kind="ExternalInput"))
        self.d_wd = ap(nc.dram_tensor("wd", [KC, 128, FF], F32, kind="ExternalInput"))
        self.d_wq = ap(nc.dram_tensor("wq", [H, 128, 2048], F32, kind="ExternalInput"))
        self.d_wk = ap(nc.dram_tensor("wk", [H, 128, 2048], F32, kind="ExternalInput"))
        self.d_wv = ap(nc.dram_tensor("wv", [H, 128, 2048], F32, kind="ExternalInput"))
        self.d_wo = ap(nc.dram_tensor("wo", [KC, 128, 2048], F32, kind="ExternalInput"))
        self.d_qa = ap(nc.dram_tensor("qa", [128, 256], F32, kind="ExternalInput"))
        self.d_ka = ap(nc.dram_tensor("ka", [128, 256], F32, kind="ExternalInput"))
        self.d_va = ap(nc.dram_tensor("va", [128, 256], F32, kind="ExternalInput"))
        self.d_oa = ap(nc.dram_tensor("oa", [128, 256], F32, kind="ExternalInput"))
        self.d_qb = ap(nc.dram_tensor("qb", [16, 2048], F32, kind="ExternalInput"))
        self.d_kb = ap(nc.dram_tensor("kb", [16, 2048], F32, kind="ExternalInput"))
        self.d_vb = ap(nc.dram_tensor("vb", [16, 2048], F32, kind="ExternalInput"))
        self.d_ob = ap(nc.dram_tensor("ob", [16, 2048], F32, kind="ExternalInput"))
        self.d_gam = ap(nc.dram_tensor("gam", [128, 16], F32, kind="ExternalInput"))
        self.d_cos = ap(nc.dram_tensor("cosp", [128, T], F32, kind="ExternalInput"))
        self.d_sin = ap(nc.dram_tensor("sinp", [128, T], F32, kind="ExternalInput"))
        self.d_deny = ap(nc.dram_tensor("deny", [128, 40], F32, kind="ExternalInput"))
        self.d_mh = ap(nc.dram_tensor("mh", [128, 128], F32, kind="ExternalInput"))
        self.d_md = ap(nc.dram_tensor("md", [128, 128], F32, kind="ExternalInput"))
        self.d_ident = ap(nc.dram_tensor("ident", [128, 128], F32, kind="ExternalInput"))
        self.d_wgs = ap(nc.dram_tensor("wgs", [FFC, 128, 2048], BF16, kind="Internal"))
        self.d_wus = ap(nc.dram_tensor("wus", [FFC, 128, 2048], BF16, kind="Internal"))
        self.d_wds = ap(nc.dram_tensor("wds", [KC, 128, FF], BF16, kind="Internal"))
        self.dbg = "dbg" in stages
        self.dbg2 = "dbga" in stages
        if self.dbg or self.dbg2:
            self.d_dbg = ap(nc.dram_tensor("dbg", [128, 5632], F32, kind="ExternalOutput"))

        self.A = nc.alloc_sbuf_tensor("arena", [128, ARENA_BYTES // 2], BF16)
        self.A32 = self.A.bitcast(F32)
        assert nc.sbuf_bytes_remaining >= 0

        self.PB = [nc.alloc_psum_tensor(f"pb{i}", [128, 512], F32) for i in range(8)]
        self.PBb = [pb.bitcast(BF16) for pb in self.PB]
        self.bank_wr = [None] * 8
        self.bank_rd = [None] * 8

        self.sem_t = Sem(nc.semaphore("semt").__enter__())
        self.sem_v = Sem(nc.semaphore("semv").__enter__())
        self.sem_s = Sem(nc.semaphore("sems").__enter__())
        self.sem_g = Sem(nc.semaphore("semg").__enter__())
        mk = lambda nm: Sem(nc.semaphore(nm).__enter__())
        self.sem_ws = [mk(f"ws{i}") for i in range(4)]
        self.sem_ww = [mk(f"ww{i}") for i in range(4)]
        self.sem_ds = [mk(f"ds{i}") for i in range(8)]
        self.sem_dw = [mk(f"dw{i}") for i in range(8)]
        self.sem_x = [mk(f"sx{i}") for i in range(2)]
        self.sem_y = [mk(f"sy{i}") for i in range(2)]
        self.sem_c1 = mk("sc1")
        self.sem_c2 = mk("sc2")
        self.sem_cs = mk("scs")
        self.sem_la = mk("sla")
        self.sem_lb = {(n, j): mk(f"slb{n}{j}") for n in "qkv" for j in range(2)}
        self.sem_oa = mk("soa")
        self.sem_ob = [mk(f"sob{i}") for i in range(2)]
        self.sem_dbg = mk("sdbg")
        self.ww_last = [None] * 4
        self.dw_last = [None] * 8

        self.q = {k: [] for k in ("tensor", "vector", "scalar", "gpsimd", "sync")}
        self.waited = {k: {} for k in self.q}

    def fsl(self, off, n, p0=0, p1=128):
        assert off % 4 == 0
        return self.A32[p0:p1, off // 4: off // 4 + n]

    def bsl(self, off, n, p0=0, p1=128):
        assert off % 2 == 0
        return self.A[p0:p1, off // 2: off // 2 + n]

    def x_own(self, c, t0=0, t1=TOK):
        return self.A32[:, (R_X + c * 4096) // 4 + t0: (R_X + c * 4096) // 4 + t1]

    def x_halo(self, c, t0=0, t1=HALO):
        return self.A32[:, (R_XH + c * 2048) // 4 + t0: (R_XH + c * 2048) // 4 + t1]

    def hv_c(self, c, t0=0, t1=T):
        return self.A[:, (R_HV + c * 3072) // 2 + t0: (R_HV + c * 3072) // 2 + t1]

    def xhat(self, c):
        return self.A[:, (R_HV + c * 1024) // 2: (R_HV + c * 1024) // 2 + SQ]

    def h_ffn(self, m):
        off = (R_H + m * 1024) if m < 32 else (R_HV + 16384 + (m - 32) * 1024)
        return self.A[:, off // 2: off // 2 + SQ]

    def qa_sl(self, h, t0=0, t1=TOK):
        return self.A[:, (R_H + h * 2048) // 2 + t0: (R_H + h * 2048) // 2 + t1]

    def wr_slot(self, i):
        return self.A[:, (R_WR + i * 4096) // 2: (R_WR + (i + 1) * 4096) // 2]

    def d_slot(self, i):
        return self.A[:, (R_WR + i * 2048) // 2: (R_WR + i * 2048) // 2 + 1024]

    def on(self, eng, build, waits=(), inc=None, k=1):
        ws = []
        wd = self.waited[eng]
        for ev in waits:
            if ev is None:
                continue
            s, v = ev
            assert v <= s.n
            if wd.get(id(s), 0) < v:
                wd[id(s)] = v
                ws.append((s, v))
        ev_out = None
        if inc is not None:
            inc.n += k
            ev_out = (inc, inc.n)

        def thunk(e, build=build, ws=tuple(ws), inc=inc, k=k):
            for s, v in ws:
                e.wait_ge(s.h, v)
            i = build(e)
            if inc is not None:
                i.then_inc(inc.h, k)
        self.q[eng].append(thunk)
        return ev_out

    def dma(self, qeng, out, in_, waits=(), sem=None):
        assert sem is not None
        return self.on(qeng, lambda e, o=out, i=in_: e.dma_start(out=o, in_=i),
                       waits, inc=sem, k=16)

    def _rd_evs(self, bank):
        r = self.bank_rd[bank]
        if r is None:
            return []
        return list(r) if isinstance(r, list) else [r]

    def mmg(self, bank, out_ap, parts, waits=(), start=True, stop=True):
        wfirst = list(waits)
        if start:
            wfirst += self._rd_evs(bank)
        n = len(parts)
        ev = None
        for i, (l, r, w) in enumerate(parts):
            allw = (wfirst + list(w)) if i == 0 else list(w)
            ev = self.on("tensor",
                         lambda e, o=out_ap, l=l, r=r, st=(start and i == 0), sp=(stop and i == n - 1):
                         e.matmul(o, l, r, start=st, stop=sp),
                         allw, inc=(self.sem_t if i == n - 1 else None))
        self.bank_wr[bank] = ev
        return ev

    def tpose(self, bank, out_ap, in_ap, ident, waits=()):
        w = list(waits) + self._rd_evs(bank)
        ev = self.on("tensor", lambda e, o=out_ap, i=in_ap, d=ident: e.transpose(o, i, d),
                     w, inc=self.sem_t)
        self.bank_wr[bank] = ev
        return ev

    def bank_read(self, bank, ev):
        self.bank_rd[bank] = ev

    def bcast(self, cols_ap, groups, dsts, bt_pair, bc_pair, rows2, wev):
        u = 0
        evs_out = []
        for gi, grp in enumerate(groups):
            bank = bc_pair[gi % 2]
            evb = None
            for k, cidx in enumerate(grp):
                bt = bt_pair[u % 2]
                row = rows2[u % 2]
                evt = self.tpose(bt, self.PB[bt][0:1, 0:128], cols_ap[:, cidx:cidx + 1],
                                 self.idf, waits=[wev, self.ev_id])
                evv = self.on("vector", lambda e, o=row, i=self.PB[bt][0:1, 0:128]:
                              e.tensor_copy(o, i), [evt], inc=self.sem_v)
                self.bank_read(bt, evv)
                pslice = self.PB[bank][:, k * 128:(k + 1) * 128]
                w0 = self._rd_evs(bank) if k == 0 else []
                evb = self.on("tensor",
                              lambda e, o=pslice, l=self.on1, r=row, st=(k == 0),
                              sp=(k == len(grp) - 1): e.matmul(o, l, r, start=st, stop=sp),
                              w0 + [evv, self.ev_on1],
                              inc=(self.sem_t if k == len(grp) - 1 else None))
                u += 1
            self.bank_wr[bank] = evb
            eve = self.on("scalar", lambda e, o=dsts[gi], i=self.PB[bank][:, 0:128 * len(grp)]:
                          e.copy(o, i), [evb], inc=self.sem_s)
            self.bank_read(bank, eve)
            evs_out.append(eve)
        return evs_out

    def tguard(self):
        return (self.sem_t, self.sem_t.n)

    def vguard(self):
        return (self.sem_v, self.sem_v.n)

    def build(self):
        nc = self.nc
        idf = self.fsl(S_IDF, 128)
        idb = self.bsl(S_IDB, 128)
        on1 = self.fsl(S_ON1, 128, 0, 1)
        onb = self.bsl(S_ONB, 1)
        gam = self.fsl(S_GAM, 16)
        deny = self.fsl(S_DEN, 40)
        mh = self.bsl(S_MH, 128)
        md = self.bsl(S_MD, 128)
        self.idf, self.idb, self.on1, self.onb = idf, idb, on1, onb
        self.gam, self.deny, self.mh, self.md = gam, deny, mh, md

        self.dma("sync", idf, self.d_ident, sem=self.sem_c1)
        self.dma("sync", gam, self.d_gam, sem=self.sem_c1)
        self.dma("sync", deny, self.d_deny, sem=self.sem_c1)
        grp1 = (self.sem_c1, self.sem_c1.n)
        self.ev_id = self.ev_gam = self.ev_deny = grp1
        self.dma("gpsimd", idb, self.d_ident, sem=self.sem_c2)
        self.dma("gpsimd", mh, self.d_mh, sem=self.sem_c2)
        self.dma("gpsimd", md, self.d_md, sem=self.sem_c2)
        grp2 = (self.sem_c2, self.sem_c2.n)
        self.ev_idb = self.ev_mh = self.ev_md = grp2
        self.ev_on1 = self.on("vector", lambda e: e.memset(on1, 1.0), inc=self.sem_v)
        self.ev_onb = self.on("vector", lambda e: e.memset(onb, 1.0), inc=self.sem_v)

        cvals = [EPS_LN, EPS_RMS, -0.5, 1.0 / R, SCALE, 0.5, 1.0 / D]
        ev_cp = None
        for i, v in enumerate(cvals):
            capp = self.fsl(S_CP + i * 32, 1)
            ev_cp = self.on("vector", lambda e, o=capp, vv=v: e.memset(o, vv), inc=self.sem_v)
            self.nc.const_aps.register(F32, v, capp)
        self.on("scalar", lambda e: None, [ev_cp])
        self.on("vector", lambda e: None, [ev_cp])
        self.on("gpsimd", lambda e: None, [ev_cp])

        stage = [self.fsl(ST0, 2048), self.fsl(ST1, 2048)]
        st_free = [None, None]
        ntile = T // 128
        ld_ev = [None] * ntile
        for i in range(2):
            ld_ev[i] = self.dma("sync", stage[i], self.d_x[i * 128:(i + 1) * 128, :],
                                sem=self.sem_x[i])
        pb_in = (6, 7)
        for i in range(ntile):
            for c in range(KC):
                b = pb_in[c % 2]
                evt = self.tpose(b, self.PB[b][:, 0:128], stage[i % 2][:, c * 128:(c + 1) * 128],
                                 idf, waits=[ld_ev[i], self.ev_id])
                if i < 4:
                    dst = self.x_halo(c, i * 128, i * 128 + 128)
                else:
                    dst = self.x_own(c, (i - 4) * 128, (i - 4) * 128 + 128)
                evv = self.on("vector",
                              lambda e, d=dst, p=self.PB[b][:, 0:128]: e.tensor_copy(d, p),
                              [evt], inc=self.sem_v)
                self.bank_read(b, evv)
                last_t = evt
            st_free[i % 2] = last_t
            j = i + 2
            if j < ntile:
                ld_ev[j] = self.dma("sync", stage[j % 2], self.d_x[j * 128:(j + 1) * 128, :],
                                    waits=[st_free[j % 2]], sem=self.sem_x[j % 2])

        self._gates_done = None
        if "ffn1" in self.stages:
            for s in range(3):
                self.ffn_stripe(call=0, stripe=s, cast_guard=self._gates_done)
        if "attn" in self.stages:
            self.attention()
        if "ffn2" in self.stages:
            for s in range(2):
                self.ffn_stripe(call=1, stripe=s,
                                cast_guard=None if s == 0 else self._gates_done)

        gv = self.vguard()
        dma_done = [None, None]
        for i in range(8):
            sl = i % 2
            last_v = None
            for c in range(KC):
                b = pb_in[c % 2]
                evt = self.tpose(b, self.PB[b][:, 0:128],
                                 self.x_own(c, i * 128, i * 128 + 128), idf,
                                 waits=[self.ev_id, gv, dma_done[sl]])
                evv = self.on("vector",
                              lambda e, d=stage[sl][:, c * 128:(c + 1) * 128], p=self.PB[b][:, 0:128]:
                              e.tensor_copy(d, p), [evt], inc=self.sem_v)
                self.bank_read(b, evv)
                last_v = evv
            dma_done[sl] = self.dma("scalar", self.d_y[i * 128:(i + 1) * 128, :], stage[sl],
                                    waits=[last_v], sem=self.sem_y[sl])
        self.on("scalar", lambda e: None, [dma_done[0], dma_done[1]])

        with nc.Block() as blk:
            blk.tensor(lambda e: [f(e) for f in self.q["tensor"]])
            blk.vector(lambda e: [f(e) for f in self.q["vector"]])
            blk.scalar(lambda e: [f(e) for f in self.q["scalar"]])
            blk.gpsimd(lambda e: [f(e) for f in self.q["gpsimd"]])
            blk.sync(lambda e: [f(e) for f in self.q["sync"]])
        return nc

    def ffn_stripe(self, call, stripe, cast_guard=None):
        first = (call == 0 and stripe == 0)
        if call == 0 and stripe == 0:
            xs = [self.x_halo(c) for c in range(KC)]
        elif call == 0:
            t0 = (stripe - 1) * SQ
            xs = [self.x_own(c, t0, t0 + SQ) for c in range(KC)]
        else:
            t0 = stripe * SQ
            xs = [self.x_own(c, t0, t0 + SQ) for c in range(KC)]

        mean_bc = self.fsl(FS_MEAN, 512)
        rstd_bc = self.fsl(FS_RSTD, 512)
        sg2 = [self.fsl(FS_SG[0], 512), self.fsl(FS_SG[1], 512)]
        tmp = self.fsl(FS_TMP, 512)
        cols = self.fsl(FS_COLS, 32)
        rows2 = [self.fsl(FS_ROW0, 128, 0, 1), self.fsl(FS_ROW1, 128, 0, 1)]
        dump2 = [self.fsl(FS_DUMP, 128), self.fsl(FS_DUMP + 512, 128)]
        idf, on1, onb = self.idf, self.on1, self.onb
        gt = self.tguard()
        gs = (self.sem_s, self.sem_s.n)
        cg = cast_guard if cast_guard is not None else gt

        cast_ev = []
        for c in range(KC):
            cast_ev.append(self.on("vector",
                                   lambda e, d=self.xhat(c), s=xs[c]: e.tensor_copy(d, s),
                                   [cg, gs], inc=self.sem_v))
        BXXs, BSUMs = (4, 0), (5, 1)
        stat_evs = []
        for tt in range(4):
            bx = BXXs[tt % 2]
            bs = BSUMs[tt % 2]
            pxx = self.PB[bx][:, 0:128]
            psm = self.PB[bs][:, 0:1]
            parts = [(self.xhat(c)[:, tt * 128:(tt + 1) * 128],
                      self.xhat(c)[:, tt * 128:(tt + 1) * 128], [cast_ev[c]])
                     for c in range(KC)]
            evx = self.mmg(bx, pxx, parts)
            parts = [(self.xhat(c)[:, tt * 128:(tt + 1) * 128], onb, [cast_ev[c], self.ev_onb])
                     for c in range(KC)]
            evs = self.mmg(bs, psm, parts)
            dmp = dump2[tt % 2]
            ev1 = self.on("vector",
                          lambda e, o=dmp, i0=pxx, i1=idf: e.tensor_tensor(o, i0, i1, ALU.mult),
                          [evx, self.ev_id], inc=self.sem_v)
            self.bank_read(bx, ev1)
            ev1 = self.on("vector",
                          lambda e, o=cols[:, 4 + tt:5 + tt], i=dmp:
                          e.reduce_sum(o, i, axis=AX.X), [ev1], inc=self.sem_v)
            ev2 = self.on("vector",
                          lambda e, o=cols[:, tt:tt + 1], i=psm:
                          e.tensor_scalar(o, i, 1.0 / D, None, ALU.mult),
                          [evs], inc=self.sem_v)
            self.bank_read(bs, ev2)
            stat_evs += [ev1, ev2]
        ev = self.on("vector", lambda e, o=dump2[0][:, 0:4], a=cols[:, 0:4]:
                     e.tensor_tensor(o, a, a, ALU.mult), stat_evs, inc=self.sem_v)
        ev = self.on("vector", lambda e, o=cols[:, 4:8], b=dump2[0][:, 0:4]:
                     e.scalar_tensor_tensor(o, o, 1.0 / D, b, ALU.mult, ALU.subtract),
                     [ev], inc=self.sem_v)
        ev = self.on("scalar", lambda e, o=cols[:, 4:8]: e.activation(o, o, AF.Ln, bias=EPS_LN),
                     [ev], inc=self.sem_s)
        ev_rs = self.on("scalar", lambda e, o=cols[:, 4:8]: e.activation(o, o, AF.Exp, scale=-0.5),
                        [ev], inc=self.sem_s)
        ev_bc = self.bcast(cols, [list(range(0, 4)), list(range(4, 8))],
                           [mean_bc, rstd_bc], (2, 3), (4, 5), rows2, ev_rs)
        gt2 = self.tguard()
        app_ev = []
        for c in range(KC):
            e1 = self.on("vector", lambda e, o=tmp, a=xs[c], b=mean_bc: e.tensor_sub(o, a, b),
                         ev_bc + [gt2], inc=self.sem_v)
            e2 = self.on("vector", lambda e, o=self.xhat(c), a=tmp, b=rstd_bc: e.tensor_mul(o, a, b),
                         [e1], inc=self.sem_v)
            app_ev.append(e2)

        ring_loaded = [None] * 4
        ring_free = [gt] * 4
        if first:
            ring_written = [[] for _ in range(4)]
        else:
            ring_written = [[self.ww_last[sl], self.dw_last[2 * sl], self.dw_last[2 * sl + 1]]
                            for sl in range(4)]

        def load_tile(j):
            m = j // 2
            sl = j % 4
            slot = self.wr_slot(sl)
            if first:
                src = self.d_wg[m] if j % 2 == 0 else self.d_wu[m]
                ev = self.dma("gpsimd", slot, src, waits=[ring_free[sl]] + ring_written[sl],
                              sem=self.sem_ws[sl])
                ring_loaded[sl] = ev
                dst = self.d_wgs[m] if j % 2 == 0 else self.d_wus[m]
                evw = self.dma("scalar", dst, slot, waits=[ev], sem=self.sem_ww[sl])
                ring_written[sl] = [evw]
                self.ww_last[sl] = evw
            else:
                ring_loaded[sl] = self.dma("sync", slot, src_scr(j),
                                           waits=[ring_free[sl]] + ring_written[sl],
                                           sem=self.sem_ws[sl])
                ring_written[sl] = []

        def src_scr(j):
            m = j // 2
            return self.d_wgs[m] if j % 2 == 0 else self.d_wus[m]

        for j in range(4):
            load_tile(j)
        BG = [0, 2]
        BU = [1, 3]
        hm_ev = [None] * FFC
        for m in range(FFC):
            bg = BG[m % 2]
            bu = BU[m % 2]
            slg = (2 * m) % 4
            slu = (2 * m + 1) % 4
            pg = self.PB[bg][:, 0:512]
            pu = self.PB[bu][:, 0:512]
            parts = [(self.wr_slot(slg)[:, k * 128:(k + 1) * 128], self.xhat(k),
                      [ring_loaded[slg], app_ev[k]]) for k in range(KC)]
            evg = self.mmg(bg, pg, parts)
            parts = [(self.wr_slot(slu)[:, k * 128:(k + 1) * 128], self.xhat(k),
                      [ring_loaded[slu], app_ev[k]]) for k in range(KC)]
            evu = self.mmg(bu, pu, parts)
            ring_free[slg] = evg
            ring_free[slu] = evu
            sg = sg2[m % 2]
            evsi = self.on("scalar", lambda e, o=sg, i=pg: e.activation(o, i, AF.Silu),
                           [evg] + ([hm_ev[m - 2]] if m >= 2 else []), inc=self.sem_s)
            self.bank_read(bg, evsi)
            evh = self.on("vector", lambda e, o=self.h_ffn(m), a=sg, b=pu: e.tensor_mul(o, a, b),
                          [evsi, evu], inc=self.sem_v)
            self.bank_read(bu, evh)
            hm_ev[m] = evh
            for jj in (2 * m + 4, 2 * m + 5):
                if 4 <= jj < 2 * FFC:
                    load_tile(jj)

        if self.dbg and call == 0 and stripe == 1:
            self._dbg_cap = []

        def dbg_cap(idx, src, wv):
            dst = self.A32[:, (R_XH + idx * 2048) // 4: (R_XH + idx * 2048) // 4 + 512]
            cv = self.on("vector", lambda e, o=dst, s=src: e.tensor_copy(o, s),
                         [wv], inc=self.sem_v)
            self._dbg_cap.append(self.dma("scalar", self.d_dbg[:, idx * 512:(idx + 1) * 512],
                                          dst, waits=[cv], sem=self.sem_dbg))

        sp_ev = [None] * FFC
        for s16 in range(16):
            hm = [self.h_ffn(4 * s16 + j) for j in range(4)]
            wev = [hm_ev[4 * s16 + j] for j in range(4)]
            a = [self.bsl(SP_A[j], 512) for j in range(4)]
            w = [self.bsl(SP_W[j], 512) for j in range(3)]
            aev = []
            for j in range(4):
                prevw = [sp_ev[4 * (s16 - 1) + j]] if s16 > 0 else []
                aev.append(self.on("scalar", lambda e, o=a[j], i=hm[j]: e.activation(o, i, AF.Abs),
                                   [wev[j]] + prevw, inc=self.sem_s))
            e1 = self.on("vector", lambda e, o=w[0], x=a[0], y=a[1]: e.tensor_tensor(o, x, y, ALU.max),
                         [aev[0], aev[1]], inc=self.sem_v)
            e2 = self.on("vector", lambda e, o=w[1], x=a[2], y=a[3]: e.tensor_tensor(o, x, y, ALU.max),
                         [aev[2], aev[3]], inc=self.sem_v)
            e3 = self.on("vector", lambda e, o=w[0], x=w[0], y=w[1]: e.tensor_tensor(o, x, y, ALU.min),
                         [e1, e2], inc=self.sem_v)
            e4 = self.on("vector", lambda e, o=w[1], x=a[0], y=a[1]: e.tensor_tensor(o, x, y, ALU.min),
                         [e3], inc=self.sem_v)
            e5 = self.on("vector", lambda e, o=w[2], x=a[2], y=a[3]: e.tensor_tensor(o, x, y, ALU.min),
                         [e4], inc=self.sem_v)
            e6 = self.on("vector", lambda e, o=w[1], x=w[1], y=w[2]: e.tensor_tensor(o, x, y, ALU.max),
                         [e5], inc=self.sem_v)
            e_t = self.on("vector", lambda e, o=w[0], x=w[0], y=w[1]: e.tensor_tensor(o, x, y, ALU.max),
                          [e6], inc=self.sem_v)
            if self.dbg and call == 0 and stripe == 1 and s16 == 0:
                dbg_cap(0, hm[0], wev[0])
                dbg_cap(1, a[0], aev[0])
                dbg_cap(2, a[1], aev[1])
                dbg_cap(3, w[0], e_t)
            for j in range(4):
                e_msk = self.on("vector", lambda e, aj=a[j], t=w[0]: e.tensor_tensor(aj, aj, t, ALU.is_ge),
                                [e_t, aev[j]], inc=self.sem_v)
                if self.dbg and call == 0 and stripe == 1 and s16 == 0 and j == 0:
                    dbg_cap(4, a[0], e_msk)
                sp_ev[4 * s16 + j] = self.on("vector",
                                             lambda e, hj=hm[j], aj=a[j]: e.tensor_mul(hj, hj, aj),
                                             [e_msk], inc=self.sem_v)
                if self.dbg and call == 0 and stripe == 1 and s16 == 0 and j == 0:
                    dbg_cap(5, hm[0], sp_ev[0])

        gt3 = self.tguard()
        self._gates_done = gt3
        dslot_loaded = [None] * 8
        dslot_free = [gt3] * 8
        if first:
            dslot_written = [[self.ww_last[i // 2]] for i in range(8)]
        else:
            dslot_written = [[self.dw_last[i], self.ww_last[i // 2]] for i in range(8)]

        def load_dgroup(dc, g):
            i = (dc * 8 + g) % 8
            if first:
                ev = self.dma("gpsimd", self.d_slot(i), self.d_wd[dc][:, g * 1024:(g + 1) * 1024],
                              waits=[dslot_free[i]] + dslot_written[i], sem=self.sem_ds[i])
                dslot_loaded[i] = ev
                evw = self.dma("scalar", self.d_wds[dc][:, g * 1024:(g + 1) * 1024],
                               self.d_slot(i), waits=[ev], sem=self.sem_dw[i])
                dslot_written[i] = [evw]
                self.dw_last[i] = evw
            else:
                dslot_loaded[i] = self.dma("sync", self.d_slot(i),
                                           self.d_wds[dc][:, g * 1024:(g + 1) * 1024],
                                           waits=[dslot_free[i]] + dslot_written[i],
                                           sem=self.sem_ds[i])
                dslot_written[i] = []

        for g in range(8):
            load_dgroup(0, g)
        BD = [6, 7]
        for dc in range(KC):
            bd = BD[dc % 2]
            pd = self.PB[bd][:, 0:512]
            ev = None
            for g in range(8):
                i = (dc * 8 + g) % 8
                parts = [(self.d_slot(i)[:, ii * 128:(ii + 1) * 128], self.h_ffn(g * 8 + ii),
                          [dslot_loaded[i], sp_ev[g * 8 + ii]]) for ii in range(8)]
                ev = self.mmg(bd, pd, parts, start=(g == 0), stop=(g == 7))
                dslot_free[i] = ev
                if dc + 1 < KC:
                    load_dgroup(dc + 1, g)
            if self.dbg and call == 0 and stripe == 1 and dc == 0:
                dbg_cap(6, pd, ev)
                dbg_cap(8, self.d_slot(0)[:, 0:512], dslot_loaded[0])
                dbg_cap(9, self.h_ffn(32), sp_ev[32])
                dbg_cap(10, self.h_ffn(8), sp_ev[8])
            eve = self.on("vector",
                          lambda e, o=xs[dc], p=pd: e.scalar_tensor_tensor(o, p, 0.5, o, ALU.mult, ALU.add),
                          [ev], inc=self.sem_v)
            self.bank_read(bd, eve)
            if self.dbg and call == 0 and stripe == 1 and dc == 0:
                dbg_cap(7, xs[0], eve)

    def attention(self):
        idf, idb, on1, onb = self.idf, self.idb, self.on1, self.onb
        gam, deny, mh, md = self.gam, self.deny, self.mh, self.md
        mean_bc = self.fsl(L2_MEAN, T)
        rstd_bc = self.fsl(L2_RSTD, T)
        tmp = self.fsl(L2_TMP, T)
        cols24 = self.fsl(AT_COLS, 32)
        rows2 = [self.fsl(AT_ROW0, 128, 0, 1), self.fsl(AT_ROW1, 128, 0, 1)]
        dump2 = [self.fsl(AT_DUMP0, 128), self.fsl(AT_DUMP1, 128)]
        gt = self.tguard()
        gs = (self.sem_s, self.sem_s.n)
        cg = gt

        cast_ev = []
        for c in range(KC):
            e1 = self.on("vector", lambda e, o=self.hv_c(c, 0, HALO), s=self.x_halo(c):
                         e.tensor_copy(o, s), [cg, gs], inc=self.sem_v)
            e2 = self.on("vector", lambda e, o=self.hv_c(c, HALO, T), s=self.x_own(c):
                         e.tensor_copy(o, s), [e1], inc=self.sem_v)
            cast_ev.append(e2)

        BXXs, BSUMs = (4, 0), (5, 1)
        stat_evs = []
        for tt in range(12):
            bx = BXXs[tt % 2]
            bs = BSUMs[tt % 2]
            pxx = self.PB[bx][:, 0:128]
            psm = self.PB[bs][:, 0:1]
            parts = [(self.hv_c(c, tt * 128, tt * 128 + 128),
                      self.hv_c(c, tt * 128, tt * 128 + 128), [cast_ev[c]]) for c in range(KC)]
            evx = self.mmg(bx, pxx, parts)
            parts = [(self.hv_c(c, tt * 128, tt * 128 + 128), onb, [cast_ev[c], self.ev_onb])
                     for c in range(KC)]
            evs = self.mmg(bs, psm, parts)
            dmp = dump2[tt % 2]
            ev1 = self.on("vector",
                          lambda e, o=dmp, i0=pxx, i1=idf: e.tensor_tensor(o, i0, i1, ALU.mult),
                          [evx, self.ev_id], inc=self.sem_v)
            self.bank_read(bx, ev1)
            ev1 = self.on("vector",
                          lambda e, o=cols24[:, 12 + tt:13 + tt], i=dmp:
                          e.reduce_sum(o, i, axis=AX.X), [ev1], inc=self.sem_v)
            ev2 = self.on("vector",
                          lambda e, o=cols24[:, tt:tt + 1], i=psm:
                          e.tensor_scalar(o, i, 1.0 / D, None, ALU.mult),
                          [evs], inc=self.sem_v)
            self.bank_read(bs, ev2)
            stat_evs += [ev1, ev2]
        ev = self.on("vector", lambda e, o=dump2[0][:, 0:12], a=cols24[:, 0:12]:
                     e.tensor_tensor(o, a, a, ALU.mult), stat_evs, inc=self.sem_v)
        ev = self.on("vector", lambda e, o=cols24[:, 12:24], b=dump2[0][:, 0:12]:
                     e.scalar_tensor_tensor(o, o, 1.0 / D, b, ALU.mult, ALU.subtract),
                     [ev], inc=self.sem_v)
        ev = self.on("scalar", lambda e, o=cols24[:, 12:24]: e.activation(o, o, AF.Ln, bias=EPS_LN),
                     [ev], inc=self.sem_s)
        ev_rs = self.on("scalar", lambda e, o=cols24[:, 12:24]: e.activation(o, o, AF.Exp, scale=-0.5),
                        [ev], inc=self.sem_s)

        segs_m = [mean_bc[:, s * 512:(s + 1) * 512] for s in range(3)]
        segs_r = [rstd_bc[:, s * 512:(s + 1) * 512] for s in range(3)]
        ev_mb = self.bcast(cols24, [[0, 1, 2, 3], [4, 5, 6, 7], [8, 9, 10, 11]],
                           segs_m, (6, 7), (2, 3), rows2, ev_rs)
        ev_rb = self.bcast(cols24, [[12, 13, 14, 15], [16, 17, 18, 19], [20, 21, 22, 23]],
                           segs_r, (6, 7), (2, 3), rows2, ev_rs)
        gt2 = self.tguard()

        passa_ev = []
        for c in range(KC):
            e1 = self.on("vector", lambda e, o=tmp[:, 0:HALO], a=self.x_halo(c), b=mean_bc[:, 0:HALO]:
                         e.tensor_sub(o, a, b), ev_mb + [gt2], inc=self.sem_v)
            e2 = self.on("vector", lambda e, o=tmp[:, HALO:T], a=self.x_own(c), b=mean_bc[:, HALO:T]:
                         e.tensor_sub(o, a, b), [e1], inc=self.sem_v)
            e3 = self.on("vector", lambda e, o=tmp, b=rstd_bc: e.tensor_mul(o, o, b),
                         [e2] + ev_rb, inc=self.sem_v)
            e4 = self.on("vector", lambda e, o=self.hv_c(c), s=tmp: e.tensor_copy(o, s),
                         [e3], inc=self.sem_v)
            e5 = self.on("vector", lambda e, o=self.x_own(c), b=tmp[:, HALO:T]:
                         e.tensor_add(o, o, b), [e4], inc=self.sem_v)
            passa_ev.append(e5)

        rms_evs = []
        for tt in range(12):
            bx = BXXs[tt % 2]
            pxx = self.PB[bx][:, 0:128]
            parts = [(self.hv_c(c, tt * 128, tt * 128 + 128),
                      self.hv_c(c, tt * 128, tt * 128 + 128), [passa_ev[c]]) for c in range(KC)]
            evx = self.mmg(bx, pxx, parts)
            dmp = dump2[tt % 2]
            ev1 = self.on("vector",
                          lambda e, o=dmp, i0=pxx, i1=idf: e.tensor_tensor(o, i0, i1, ALU.mult),
                          [evx, self.ev_id], inc=self.sem_v)
            self.bank_read(bx, ev1)
            ev1 = self.on("vector",
                          lambda e, o=cols24[:, tt:tt + 1], i=dmp:
                          e.reduce_sum(o, i, axis=AX.X), [ev1], inc=self.sem_v)
            rms_evs.append(ev1)
        ev = self.on("scalar", lambda e, o=cols24[:, 0:12]:
                     e.activation(o, o, AF.Ln, bias=EPS_RMS, scale=1.0 / D),
                     rms_evs, inc=self.sem_s)
        ev_r = self.on("scalar", lambda e, o=cols24[:, 0:12]: e.activation(o, o, AF.Exp, scale=-0.5),
                       [ev], inc=self.sem_s)
        ev_rbc = self.bcast(cols24, [[0, 1, 2, 3], [4, 5, 6, 7], [8, 9, 10, 11]],
                            segs_r, (6, 7), (2, 3), rows2, ev_r)

        passb_ev = []
        for c in range(KC):
            passb_ev.append(self.on("vector",
                                    lambda e, o=self.hv_c(c), g=gam[:, c:c + 1], rb=rstd_bc:
                                    e.scalar_tensor_tensor(o, o, g, rb, ALU.mult, ALU.mult),
                                    ev_rbc + [self.ev_gam], inc=self.sem_v))

        ga = self.vguard()
        cosb = self.bsl(A_CS, T)
        sinb = self.bsl(A_SN, T)
        self.dma("gpsimd", cosb, self.d_cos, waits=[ga], sem=self.sem_cs)
        self.dma("gpsimd", sinb, self.d_sin, waits=[ga], sem=self.sem_cs)
        ev_cos = ev_sin = (self.sem_cs, self.sem_cs.n)

        qa_sb = self.bsl(AT_QA, 256)
        ka_sb = self.bsl(AT_QA + 1024, 256)
        va_sb = self.bsl(AT_QA + 2048, 256)
        self.dma("gpsimd", qa_sb, self.d_qa, waits=[gt], sem=self.sem_la)
        self.dma("gpsimd", ka_sb, self.d_ka, waits=[gt], sem=self.sem_la)
        self.dma("gpsimd", va_sb, self.d_va, waits=[gt], sem=self.sem_la)
        ev_qa = ev_ka = ev_va = (self.sem_la, self.sem_la.n)
        uq = self.bsl(FS_UQ, T, 0, 16)
        uk = self.bsl(FS_UK, T, 0, 16)
        uv = self.bsl(FS_UV, T, 0, 16)
        BL = 5
        u_ev = {}
        for nm, asb, aev, udst in (("q", qa_sb, ev_qa, uq), ("k", ka_sb, ev_ka, uk),
                                   ("v", va_sb, ev_va, uv)):
            eve = None
            for seg in range(3):
                pl = self.PB[BL][0:16, 0:512]
                parts = [(asb[:, k * 16:(k + 1) * 16], self.hv_c(k, seg * 512, (seg + 1) * 512),
                          [aev, passb_ev[k]]) for k in range(KC)]
                ev = self.mmg(BL, pl, parts)
                eve = self.on("scalar",
                              lambda e, o=udst[:, seg * 512:(seg + 1) * 512], i=pl:
                              e.activation(o, i, AF.Copy, scale=1.0 / R), [ev], inc=self.sem_s)
                self.bank_read(BL, eve)
            u_ev[nm] = eve

        lb_buf = {}
        lb_ev = {}
        for i, nm in enumerate(("q", "k", "v")):
            lb_buf[nm] = [self.bsl(A_LB + (2 * i + j) * 512, 128, 0, 16) for j in range(2)]
            lb_ev[nm] = [None, None]
        d_lb = {"q": self.d_qb, "k": self.d_kb, "v": self.d_vb}

        def load_lb(nm, h, consumed):
            sl = h % 2
            lb_ev[nm][sl] = self.dma("gpsimd", lb_buf[nm][sl], d_lb[nm][:, h * 128:(h + 1) * 128],
                                     waits=[consumed, ga], sem=self.sem_lb[(nm, sl)])
        for nm in ("q", "k", "v"):
            load_lb(nm, 0, None)
            load_lb(nm, 1, None)

        order = [(w, h) for h in range(H) for w in ("q", "k", "v")]
        wring_loaded = {}
        wring_free = [self.tguard()] * 4
        wring_written = [[self.ww_last[sl], self.dw_last[2 * sl], self.dw_last[2 * sl + 1]]
                         for sl in range(4)]
        self._widx = 0

        def pump():
            if self._widx >= len(order):
                return
            which, h = order[self._widx]
            sl = self._widx % 4
            src = {"q": self.d_wq, "k": self.d_wk, "v": self.d_wv}[which][h]
            ev = self.dma("gpsimd", self.wr_slot(sl), src,
                          waits=[wring_free[sl]] + wring_written[sl], sem=self.sem_ws[sl])
            wring_written[sl] = []
            wring_loaded[(which, h)] = (sl, ev)
            self._widx += 1

        for _ in range(4):
            pump()

        t1 = self.fsl(A_T1, 512)
        t2 = self.fsl(A_T2, 512)
        scr2 = [self.fsl(A_SC, 640), self.fsl(A_SC2, 640), self.fsl(A_SC3, 640)]
        kbf = self.bsl(A_KBF, T)
        vf = self.bsl(A_VF, T)
        pbf2 = [self.bsl(A_P, 640), self.bsl(A_P2, 640), self.bsl(A_P3, 640)]
        pt = [self.bsl(A_PT, 128), self.bsl(A_PT + 256, 128)]

        def nm_ap(p, idx):
            off = (A_NM + p * 16 + idx * 4) if p < 2 else (A_NM3 + idx * 4)
            return self.A32[:, off // 4: off // 4 + 1]

        def rope(psums, cos_ap, sin_ap, out_ap, mm_evs):
            evs = []
            for i, (bk, pp) in enumerate(psums):
                s0 = i * 512
                e1 = self.on("vector", lambda e, o=t1, a=pp, b=cos_ap[:, s0:s0 + 512]:
                             e.tensor_mul(o, a, b), [mm_evs[i], ev_cos], inc=self.sem_v)
                e2 = self.on("vector",
                             lambda e, o=t2[0:64, :], a=pp[64:128, :], b=sin_ap[0:64, s0:s0 + 512]:
                             e.tensor_mul(o, a, b), [mm_evs[i], ev_sin], inc=self.sem_v)
                e3 = self.on("vector",
                             lambda e, o=t2[64:128, :], a=pp[0:64, :], b=sin_ap[64:128, s0:s0 + 512]:
                             e.tensor_mul(o, a, b), [mm_evs[i], ev_sin], inc=self.sem_v)
                self.bank_read(bk, e3)
                e4 = self.on("vector",
                             lambda e, o=out_ap[0:64, s0:s0 + 512], a=t1[0:64, :], b=t2[0:64, :]:
                             e.tensor_sub(o, a, b), [e1, e2], inc=self.sem_v)
                e5 = self.on("vector",
                             lambda e, o=out_ap[64:128, s0:s0 + 512], a=t1[64:128, :], b=t2[64:128, :]:
                             e.tensor_add(o, a, b), [e3, e4], inc=self.sem_v)
                evs.append(e5)
            return evs

        cos_own = self.bsl(A_CS + 2 * HALO, TOK)
        sin_own = self.bsl(A_SN + 2 * HALO, TOK)

        for h in range(H):
            slq, evq = wring_loaded.pop(("q", h))
            slk, evk = wring_loaded.pop(("k", h))
            slv, evv_ = wring_loaded.pop(("v", h))
            q_banks = [(0, self.PB[0][:, 0:512]), (1, self.PB[1][:, 0:512])]
            q_mm = []
            for i, (bk, pp) in enumerate(q_banks):
                parts = [(self.wr_slot(slq)[:, k * 128:(k + 1) * 128],
                          self.hv_c(k, HALO + i * 512, HALO + (i + 1) * 512),
                          [evq, passb_ev[k]]) for k in range(KC)]
                parts.append((lb_buf["q"][h % 2], uq[:, HALO + i * 512:HALO + (i + 1) * 512],
                              [lb_ev["q"][h % 2], u_ev["q"]]))
                q_mm.append(self.mmg(bk, pp, parts))
            wring_free[slq] = q_mm[-1]
            rope_q = rope(q_banks, cos_own, sin_own, self.qa_sl(h), q_mm)
            k_banks = [(2, self.PB[2][:, 0:512]), (3, self.PB[3][:, 0:512]), (0, self.PB[0][:, 0:512])]
            k_mm = []
            for i, (bk, pp) in enumerate(k_banks):
                parts = [(self.wr_slot(slk)[:, k * 128:(k + 1) * 128],
                          self.hv_c(k, i * 512, (i + 1) * 512),
                          [evk, passb_ev[k]]) for k in range(KC)]
                parts.append((lb_buf["k"][h % 2], uk[:, i * 512:(i + 1) * 512],
                              [lb_ev["k"][h % 2], u_ev["k"]]))
                k_mm.append(self.mmg(bk, pp, parts))
            wring_free[slk] = k_mm[-1]
            rope_k = rope(k_banks, self.bsl(A_CS, T), self.bsl(A_SN, T), kbf, k_mm)
            v_banks = [(1, self.PB[1][:, 0:512]), (2, self.PB[2][:, 0:512]), (3, self.PB[3][:, 0:512])]
            v_ev = []
            for i, (bk, pp) in enumerate(v_banks):
                parts = [(self.wr_slot(slv)[:, k * 128:(k + 1) * 128],
                          self.hv_c(k, i * 512, (i + 1) * 512),
                          [evv_, passb_ev[k]]) for k in range(KC)]
                parts.append((lb_buf["v"][h % 2], uv[:, i * 512:(i + 1) * 512],
                              [lb_ev["v"][h % 2], u_ev["v"]]))
                ev = self.mmg(bk, pp, parts)
                eve = self.on("vector", lambda e, o=vf[:, i * 512:(i + 1) * 512], p=pp: e.tensor_copy(o, p),
                              [ev], inc=self.sem_v)
                self.bank_read(bk, eve)
                v_ev.append(eve)
            wring_free[slv] = v_ev[-1]
            if h + 2 < H:
                for nm in ("q", "k", "v"):
                    load_lb(nm, h + 2, {"q": q_mm[-1], "k": k_mm[-1], "v": v_ev[-1]}[nm])
            for _ in range(3):
                pump()
            BT2 = [4, 1]
            vt_ev = [None] * 12
            for jt in range(12):
                bk = BT2[jt % 2]
                evt = self.tpose(bk, self.PBb[bk][:, 0:128], vf[:, jt * 128:(jt + 1) * 128], idb,
                                 waits=[v_ev[jt // 4], self.ev_idb])
                eve = self.on("vector", lambda e, o=vf[:, jt * 128:(jt + 1) * 128], i=self.PBb[bk][:, 0:128]:
                              e.tensor_copy(o, i), [evt], inc=self.sem_v)
                self.bank_read(bk, eve)
                vt_ev[jt] = eve

            SC_BANKS = [(6, 7), (2, 3)]
            TP_BANKS = (4, 1)
            b_av = 5
            sm_ev = [None] * 8

            def emit_scores(qt):
                b_s1, b_s2 = SC_BANKS[qt % 2]
                ps1 = self.PB[b_s1][:, 0:512]
                ps2 = self.PB[b_s2][:, 0:128]
                ev_s1 = self.mmg(b_s1, ps1,
                                 [(self.qa_sl(h, qt * 128, qt * 128 + 128),
                                   kbf[:, qt * 128: qt * 128 + 512],
                                   [rope_q[-1], rope_k[-1]])])
                ev_s2 = self.mmg(b_s2, ps2,
                                 [(self.qa_sl(h, qt * 128, qt * 128 + 128),
                                   kbf[:, qt * 128 + 512: qt * 128 + 640],
                                   [rope_k[-1]])])
                return ev_s1, ev_s2

            def emit_softmax(qt, ev_s1, ev_s2):
                b_s1, b_s2 = SC_BANKS[qt % 2]
                p = qt % 3
                ps1 = self.PB[b_s1][:, 0:512]
                ps2 = self.PB[b_s2][:, 0:128]
                scr = scr2[p]
                pbf = pbf2[p]
                nmax, nm2 = nm_ap(p, 0), nm_ap(p, 1)
                rows, rinv = nm_ap(p, 2), nm_ap(p, 3)
                mev = []
                for b in range(5):
                    src = ps1[:, b * 128:(b + 1) * 128] if b < 4 else ps2
                    dcol = deny[:, qt * 5 + b: qt * 5 + b + 1]
                    dst = scr[:, b * 128:(b + 1) * 128]
                    wv = [ev_s1 if b < 4 else ev_s2, self.ev_deny]
                    if b == 0:
                        ev = self.on("vector", lambda e, o=dst, s=src, d=dcol, m=mh:
                                     e.scalar_tensor_tensor(o, s, d, m, ALU.add, ALU.add),
                                     wv + [self.ev_mh], inc=self.sem_v)
                    elif b == 4:
                        ev = self.on("vector", lambda e, o=dst, s=src, d=dcol, m=md:
                                     e.scalar_tensor_tensor(o, s, d, m, ALU.add, ALU.add),
                                     wv + [self.ev_md], inc=self.sem_v)
                    elif _SC_EVAC:
                        ev = self.on("scalar", lambda e, o=dst, s=src, d=dcol:
                                     e.activation(o, s, AF.Identity, bias=d[:, 0:1]),
                                     wv, inc=self.sem_s)
                    else:
                        ev = self.on("vector", lambda e, o=dst, s=src, d=dcol:
                                     e.tensor_scalar(o, s, d, None, ALU.add),
                                     wv, inc=self.sem_v)
                    mev.append(ev)
                self.bank_read(b_s1, [mev[1], mev[2], mev[3], mev[0]])
                self.bank_read(b_s2, mev[4])
                ev = self.on("vector", lambda e, o=nmax, s=scr: e.reduce_max(o, s, axis=AX.X, negate=True),
                             mev, inc=self.sem_v)
                ev = self.on("vector", lambda e, o=nm2, i=nmax: e.tensor_scalar(o, i, SCALE, None, ALU.mult),
                             [ev], inc=self.sem_v)
                ev = self.on("scalar", lambda e, o=pbf, s=scr, b2=nm2, r=rows:
                             e.activation(o, s, AF.Exp, bias=b2[:, 0:1], scale=SCALE, accum_out=r),
                             [ev], inc=self.sem_s)
                ev = self.on("vector", lambda e, o=rinv, i=rows: e.reciprocal(o, i), [ev], inc=self.sem_v)
                sm_ev[qt] = self.on("vector", lambda e, o=pbf, r=rinv:
                                    e.tensor_scalar(o, o, r[:, 0:1], None, ALU.mult),
                                    [ev], inc=self.sem_v)

            def emit_consume(qt):
                pbf = pbf2[qt % 3]
                pav = self.PB[b_av][:, 0:128]
                ev_av = None
                for b in range(5):
                    bk = TP_BANKS[b % 2]
                    evt = self.tpose(bk, self.PBb[bk][:, 0:128], pbf[:, b * 128:(b + 1) * 128], idb,
                                     waits=[sm_ev[qt], self.ev_idb])
                    eve = self.on("vector", lambda e, o=pt[b % 2], i=self.PBb[bk][:, 0:128]:
                                  e.tensor_copy(o, i), [evt], inc=self.sem_v)
                    self.bank_read(bk, eve)
                    w0 = self._rd_evs(b_av) if b == 0 else []
                    ev_av = self.on("tensor",
                                    lambda e, o=pav, l=vf[:, (qt + b) * 128:(qt + b + 1) * 128],
                                    r=pt[b % 2], st=(b == 0), sp=(b == 4):
                                    e.matmul(o, l, r, start=st, stop=sp),
                                    w0 + [eve, vt_ev[qt + b]], inc=self.sem_t)
                self.bank_wr[b_av] = ev_av
                eva = self.on("vector",
                              lambda e, o=self.qa_sl(h, qt * 128, qt * 128 + 128), i=pav:
                              e.tensor_copy(o, i), [ev_av], inc=self.sem_v)
                self.bank_read(b_av, eva)

            if False and self.dbg2 and h == 0:
                self._acap_last = None

                def acap(idx, src, wv):
                    dst = self.A32[:, (R_H + 28672) // 4: (R_H + 28672) // 4 + 512]
                    cv = self.on("vector", lambda e, o=dst, s=src: e.tensor_copy(o, s),
                                 [wv, self._acap_last], inc=self.sem_v)
                    self._acap_last = self.dma("scalar", self.d_dbg[:, idx * 512:(idx + 1) * 512],
                                               dst, waits=[cv], sem=self.sem_dbg)
                acap(0, self.hv_c(0, 0, 512), passb_ev[0])
                acap(1, kbf[:, 0:512], rope_k[-1])
                acap(2, vf[:, 0:512], vt_ev[3])
                acap(3, self.qa_sl(0, 0, 512), rope_q[-1])
                acap(4, self.bsl(A_CS, 512), ev_cos)
                acap(5, self.bsl(A_SN, 512), ev_sin)
                acap(6, mean_bc[:, 0:512], ev_mb[0])
                acap(7, rstd_bc[:, 0:512], ev_rbc[0])

            if _ATTN_LITE:
                pass
            elif _QT_PIPE:
                for qt in range(8):
                    es = emit_scores(qt)
                    if qt >= 2:
                        emit_consume(qt - 2)
                    emit_softmax(qt, *es)
                emit_consume(6)
                emit_consume(7)
            else:
                for qt in range(8):
                    es = emit_scores(qt)
                    emit_softmax(qt, *es)
                    emit_consume(qt)

        attn_done = self.vguard()
        oa_sb = self.bsl(O_OA, 256)
        ob_buf = [self.bsl(O_OB + j * 512, 128, 0, 16) for j in range(2)]
        uo = self.bsl(O_UO, TOK, 0, 16)
        ev_oa = self.dma("gpsimd", oa_sb, self.d_oa, waits=[attn_done], sem=self.sem_oa)
        ev_uo = None
        for seg in range(2):
            pl = self.PB[BL][0:16, 0:512]
            parts = [(oa_sb[:, k * 16:(k + 1) * 16], self.qa_sl(k, seg * 512, (seg + 1) * 512),
                      [ev_oa, attn_done]) for k in range(KC)]
            ev = self.mmg(BL, pl, parts)
            ev_uo = self.on("scalar", lambda e, o=uo[:, seg * 512:(seg + 1) * 512], i=pl:
                            e.activation(o, i, AF.Copy, scale=1.0 / R), [ev], inc=self.sem_s)
            self.bank_read(BL, ev_uo)
        oload = [None] * KC
        ob_ev = [None] * KC
        ob_consumed = [None] * KC

        def load_o(m, wait_ev):
            sl = m % 4
            oload[m] = self.dma("gpsimd", self.wr_slot(sl), self.d_wo[m],
                                waits=[wring_free[sl] if m < 4 else wait_ev],
                                sem=self.sem_ws[sl])
            ob_ev[m] = self.dma("gpsimd", ob_buf[m % 2], self.d_ob[:, m * 128:(m + 1) * 128],
                                waits=[attn_done if m < 2 else ob_consumed[m - 2]],
                                sem=self.sem_ob[m % 2])

        for m in range(2):
            load_o(m, None)
        BO = [0, 1]
        for m in range(KC):
            bo = BO[m % 2]
            po = self.PB[bo][:, 0:512]
            ev = None
            for seg in range(2):
                parts = [(self.wr_slot(m % 4)[:, k * 128:(k + 1) * 128],
                          self.qa_sl(k, seg * 512, (seg + 1) * 512),
                          [oload[m], attn_done]) for k in range(KC)]
                parts.append((ob_buf[m % 2], uo[:, seg * 512:(seg + 1) * 512],
                              [ob_ev[m], ev_uo]))
                ev = self.mmg(bo, po, parts)
                eve = self.on("vector",
                              lambda e, o=self.x_own(m, seg * 512, (seg + 1) * 512), p=po:
                              e.tensor_tensor(o, o, p, ALU.add), [ev], inc=self.sem_v)
                self.bank_read(bo, eve)
            ob_consumed[m] = ev
            if m + 2 < KC:
                load_o(m + 2, ev if m + 2 >= 4 else None)


_CACHE = {}


def _perm():
    s = np.arange(16)[:, None, None]
    j = np.arange(4)[None, :, None]
    g = np.arange(128)[None, None, :]
    new = (512 * s + 128 * j + g).ravel()
    old = (512 * s + 4 * g + j).ravel()
    p = np.empty(FF, dtype=np.int64)
    p[new] = old
    return p


def _host_weights(inp):
    perm = _perm()
    wg = np.ascontiguousarray(np.asarray(inp["L0_ffn_w_gate"], np.float32))[:, perm]
    wu = np.ascontiguousarray(np.asarray(inp["L0_ffn_w_up"], np.float32))[:, perm]
    wd = np.ascontiguousarray(np.asarray(inp["L0_ffn_w_down"], np.float32))[perm, :]
    wg_h = np.ascontiguousarray(wg.reshape(16, 128, 64, 128).transpose(2, 1, 0, 3).reshape(64, 128, 2048))
    wu_h = np.ascontiguousarray(wu.reshape(16, 128, 64, 128).transpose(2, 1, 0, 3).reshape(64, 128, 2048))
    wd_h = np.ascontiguousarray(wd.reshape(64, 128, 16, 128).transpose(2, 1, 0, 3).reshape(16, 128, 8192))

    def projT(w):
        wt = np.ascontiguousarray(np.asarray(w, np.float32).T)
        return np.ascontiguousarray(wt.reshape(16, 128, 16, 128).transpose(2, 1, 0, 3).reshape(16, 128, 2048))

    def loraA(a):
        return np.ascontiguousarray(np.asarray(a, np.float32).T.reshape(16, 128, 16)
                                    .transpose(1, 0, 2).reshape(128, 256))

    return dict(
        wg=wg_h, wu=wu_h, wd=wd_h,
        wq=projT(inp["L0_attn_wq"]), wk=projT(inp["L0_attn_wk"]),
        wv=projT(inp["L0_attn_wv"]), wo=projT(inp["L0_attn_wo"]),
        qa=loraA(inp["L0_attn_qA"]), ka=loraA(inp["L0_attn_kA"]),
        va=loraA(inp["L0_attn_vA"]), oa=loraA(inp["L0_attn_oA"]),
        qb=np.ascontiguousarray(np.asarray(inp["L0_attn_qB"], np.float32).T),
        kb=np.ascontiguousarray(np.asarray(inp["L0_attn_kB"], np.float32).T),
        vb=np.ascontiguousarray(np.asarray(inp["L0_attn_vB"], np.float32).T),
        ob=np.ascontiguousarray(np.asarray(inp["L0_attn_oB"], np.float32).T),
        gam=np.ascontiguousarray(np.asarray(inp["L0_attn_norm_w"], np.float32).reshape(16, 128).T),
    )


def _host_tables(core):
    pos = np.arange(core * TOK - HALO, core * TOK + TOK, dtype=np.int64)
    pos = np.maximum(pos, 0).astype(np.float32)
    half = DH // 2
    theta = (1.0 / (10000.0 ** (np.arange(half, dtype=np.float32) / np.float32(half)))).astype(np.float32)
    freqs = pos[None, :] * theta[:, None]
    freqs = np.concatenate([freqs, freqs], axis=0)
    cos = np.cos(freqs).astype(np.float32)
    sin = np.sin(freqs).astype(np.float32)
    deny = np.zeros((40,), dtype=np.float32)
    if core == 0:
        for qt in range(8):
            for b in range(5):
                if (qt + b) * 128 + 128 <= HALO:
                    deny[qt * 5 + b] = NEG
    deny_h = np.ascontiguousarray(np.broadcast_to(deny[None, :], (128, 40)))
    ii = np.arange(128)[:, None]
    jj = np.arange(128)[None, :]
    mhm = np.where(jj >= ii + 1, 0.0, NEG).astype(np.float32)
    mdm = np.where(jj <= ii, 0.0, NEG).astype(np.float32)
    return cos, sin, deny_h, mhm, mdm


_STAGES = ("ffn1", "attn", "ffn2")
_QT_PIPE = True
_ATTN_LITE = False
_SC_EVAC = False


def _get_nc(stages):
    if stages not in _CACHE:
        _CACHE[stages] = Builder(stages).build()
    return _CACHE[stages]


def kernel(**inputs):
    nc = _get_nc(_STAGES)
    w = _host_weights(inputs)
    x = np.ascontiguousarray(np.asarray(inputs["L0_attn_x"], dtype=np.float32).reshape(8192, D))
    ident = np.eye(128, dtype=np.float32)
    in_maps = []
    for c in range(NCORES):
        lo = c * TOK - HALO
        if lo < 0:
            xs = np.concatenate([np.zeros((-lo, D), np.float32), x[0:c * TOK + TOK]], axis=0)
        else:
            xs = x[lo: c * TOK + TOK]
        cos, sin, deny, mhm, mdm = _host_tables(c)
        in_maps.append(dict(x=np.ascontiguousarray(xs), cosp=cos, sinp=sin, deny=deny,
                            mh=mhm, md=mdm, ident=ident, **w))
    res = run_bass_kernel_spmd(nc, in_maps, core_ids=list(range(NCORES)))
    _LAST["res"] = res
    y = np.concatenate([res.results[c]["y"] for c in range(NCORES)], axis=0)
    return y.reshape(1, 8192, D).astype(np.float32)


_LAST = {}


# version 05
# speedup vs baseline: 1.0469x; 1.0469x over previous
import numpy as np
import ml_dtypes
import concourse.bass as bass
import concourse.mybir as mybir
from concourse.bass_utils import run_bass_kernel_spmd

F32 = mybir.dt.float32
BF16 = mybir.dt.bfloat16
AF = mybir.ActivationFunctionType
ALU = mybir.AluOpType
AX = mybir.AxisListType

NCORES = 8
D = 2048
KC = 16
H = 16
DH = 128
R = 16
WIN = 512
TOK = 1024
HALO = 512
T = TOK + HALO
FF = 8192
FFC = 64
SQ = 512
EPS_LN = 1e-5
EPS_RMS = 1e-6
SCALE = float(1.0 / np.sqrt(DH))
NEG = -1.0e9

R_X = 0
R_XH = 65536
R_HV = 98304
R_H = 147456
R_WR = 180224
R_FS = 196608
R_SM = 207872
ARENA_BYTES = 209920 + 256

A_T1 = R_XH + 0
A_T2 = R_XH + 2048
A_SC = R_XH + 4096
A_SC2 = R_XH + 6656
A_KBF = R_XH + 9216
A_VF = R_XH + 12288
A_P = R_XH + 15360
A_P2 = R_XH + 16640
A_PT = R_XH + 17920
A_CS = R_XH + 18432
A_SN = R_XH + 21504
A_NM = R_XH + 24576
A_LB = R_XH + 24608
A_SC3 = R_XH + 27680
A_P3 = R_XH + 30240
A_NM3 = R_XH + 31520
O_OA = A_CS + 0
O_OB = A_CS + 1024
O_UO = A_CS + 2048

S_IDF = R_SM + 0
S_IDB = R_SM + 512
S_ON1 = R_SM + 768
S_ONB = R_SM + 1280
S_GAM = R_SM + 1312
S_DEN = R_SM + 1376
S_MH = R_SM + 1536
S_MD = R_SM + 1792
S_CP = R_SM + 2048

FS_MEAN = R_FS + 0
FS_RSTD = R_FS + 2048
FS_TMP = R_FS + 6144
FS_COLS = R_FS + 4096
FS_ROW0 = R_FS + 4224
FS_ROW1 = R_FS + 4736
FS_DUMP = R_FS + 5248
FS_SG = [R_FS + 0, R_FS + 2048]
SP_A = [R_FS + 4096, R_FS + 5120, R_FS + 6144, R_FS + 7168]
SP_W = [R_FS + 8192, R_FS + 9216, R_FS + 10240]
FS_UQ = R_FS + 0
FS_UK = R_FS + 3072
FS_UV = R_FS + 6144

L2_MEAN = R_H + 0
L2_RSTD = R_H + 6144
L2_TMP = R_H + 12288
ST0 = R_H + 0
ST1 = R_H + 8192
AT_COLS = R_H + 18432
AT_ROW0 = AT_COLS + 128
AT_ROW1 = AT_ROW0 + 512
AT_DUMP0 = AT_ROW1 + 512
AT_DUMP1 = AT_DUMP0 + 512
AT_QA = AT_DUMP1 + 512


class Sem:
    __slots__ = ("h", "n")

    def __init__(self, h):
        self.h = h
        self.n = 0


class Builder:
    def __init__(self, stages=("ffn1", "attn", "ffn2")):
        self.stages = stages
        nc = bass.Bass("TRN2", target_bir_lowering=False)
        self.nc = nc
        ap = lambda t: t.ap()
        self.d_x = ap(nc.dram_tensor("x", [T, D], F32, kind="ExternalInput"))
        self.d_y = ap(nc.dram_tensor("y", [TOK, D], F32, kind="ExternalOutput"))
        self.d_wg = ap(nc.dram_tensor("wg", [FFC, 128, 2048], F32, kind="ExternalInput"))
        self.d_wu = ap(nc.dram_tensor("wu", [FFC, 128, 2048], F32, kind="ExternalInput"))
        self.d_wd = ap(nc.dram_tensor("wd", [KC, 128, FF], F32, kind="ExternalInput"))
        self.d_wq = ap(nc.dram_tensor("wq", [H, 128, 2048], F32, kind="ExternalInput"))
        self.d_wk = ap(nc.dram_tensor("wk", [H, 128, 2048], F32, kind="ExternalInput"))
        self.d_wv = ap(nc.dram_tensor("wv", [H, 128, 2048], F32, kind="ExternalInput"))
        self.d_wo = ap(nc.dram_tensor("wo", [KC, 128, 2048], F32, kind="ExternalInput"))
        self.d_qa = ap(nc.dram_tensor("qa", [128, 256], F32, kind="ExternalInput"))
        self.d_ka = ap(nc.dram_tensor("ka", [128, 256], F32, kind="ExternalInput"))
        self.d_va = ap(nc.dram_tensor("va", [128, 256], F32, kind="ExternalInput"))
        self.d_oa = ap(nc.dram_tensor("oa", [128, 256], F32, kind="ExternalInput"))
        self.d_qb = ap(nc.dram_tensor("qb", [16, 2048], F32, kind="ExternalInput"))
        self.d_kb = ap(nc.dram_tensor("kb", [16, 2048], F32, kind="ExternalInput"))
        self.d_vb = ap(nc.dram_tensor("vb", [16, 2048], F32, kind="ExternalInput"))
        self.d_ob = ap(nc.dram_tensor("ob", [16, 2048], F32, kind="ExternalInput"))
        self.d_gam = ap(nc.dram_tensor("gam", [128, 16], F32, kind="ExternalInput"))
        self.d_cos = ap(nc.dram_tensor("cosp", [128, T], F32, kind="ExternalInput"))
        self.d_sin = ap(nc.dram_tensor("sinp", [128, T], F32, kind="ExternalInput"))
        self.d_deny = ap(nc.dram_tensor("deny", [128, 40], F32, kind="ExternalInput"))
        self.d_mh = ap(nc.dram_tensor("mh", [128, 128], F32, kind="ExternalInput"))
        self.d_md = ap(nc.dram_tensor("md", [128, 128], F32, kind="ExternalInput"))
        self.d_ident = ap(nc.dram_tensor("ident", [128, 128], F32, kind="ExternalInput"))
        self.d_wgs = ap(nc.dram_tensor("wgs", [FFC, 128, 2048], BF16, kind="Internal"))
        self.d_wus = ap(nc.dram_tensor("wus", [FFC, 128, 2048], BF16, kind="Internal"))
        self.d_wds = ap(nc.dram_tensor("wds", [KC, 128, FF], BF16, kind="Internal"))
        self.dbg = "dbg" in stages
        self.dbg2 = "dbga" in stages
        if self.dbg or self.dbg2:
            self.d_dbg = ap(nc.dram_tensor("dbg", [128, 5632], F32, kind="ExternalOutput"))

        self.A = nc.alloc_sbuf_tensor("arena", [128, ARENA_BYTES // 2], BF16)
        self.A32 = self.A.bitcast(F32)
        assert nc.sbuf_bytes_remaining >= 0

        self.PB = [nc.alloc_psum_tensor(f"pb{i}", [128, 512], F32) for i in range(8)]
        self.PBb = [pb.bitcast(BF16) for pb in self.PB]
        self.bank_wr = [None] * 8
        self.bank_rd = [None] * 8

        self.sem_t = Sem(nc.semaphore("semt").__enter__())
        self.sem_v = Sem(nc.semaphore("semv").__enter__())
        self.sem_s = Sem(nc.semaphore("sems").__enter__())
        self.sem_g = Sem(nc.semaphore("semg").__enter__())
        mk = lambda nm: Sem(nc.semaphore(nm).__enter__())
        self.sem_ws = [mk(f"ws{i}") for i in range(4)]
        self.sem_ww = [mk(f"ww{i}") for i in range(4)]
        self.sem_ds = [mk(f"ds{i}") for i in range(8)]
        self.sem_dw = [mk(f"dw{i}") for i in range(8)]
        self.sem_x = [mk(f"sx{i}") for i in range(2)]
        self.sem_y = [mk(f"sy{i}") for i in range(2)]
        self.sem_c1 = mk("sc1")
        self.sem_c2 = mk("sc2")
        self.sem_cs = mk("scs")
        self.sem_la = mk("sla")
        self.sem_lb = {(n, j): mk(f"slb{n}{j}") for n in "qkv" for j in range(2)}
        self.sem_oa = mk("soa")
        self.sem_ob = [mk(f"sob{i}") for i in range(2)]
        self.sem_dbg = mk("sdbg")
        self.ww_last = [None] * 4
        self.dw_last = [None] * 8

        self.q = {k: [] for k in ("tensor", "vector", "scalar", "gpsimd", "sync")}
        self.waited = {k: {} for k in self.q}

    def fsl(self, off, n, p0=0, p1=128):
        assert off % 4 == 0
        return self.A32[p0:p1, off // 4: off // 4 + n]

    def bsl(self, off, n, p0=0, p1=128):
        assert off % 2 == 0
        return self.A[p0:p1, off // 2: off // 2 + n]

    def x_own(self, c, t0=0, t1=TOK):
        return self.A32[:, (R_X + c * 4096) // 4 + t0: (R_X + c * 4096) // 4 + t1]

    def x_halo(self, c, t0=0, t1=HALO):
        return self.A32[:, (R_XH + c * 2048) // 4 + t0: (R_XH + c * 2048) // 4 + t1]

    def hv_c(self, c, t0=0, t1=T):
        return self.A[:, (R_HV + c * 3072) // 2 + t0: (R_HV + c * 3072) // 2 + t1]

    def xhat(self, c):
        return self.A[:, (R_HV + c * 1024) // 2: (R_HV + c * 1024) // 2 + SQ]

    def h_ffn(self, m):
        off = (R_H + m * 1024) if m < 32 else (R_HV + 16384 + (m - 32) * 1024)
        return self.A[:, off // 2: off // 2 + SQ]

    def qa_sl(self, h, t0=0, t1=TOK):
        return self.A[:, (R_H + h * 2048) // 2 + t0: (R_H + h * 2048) // 2 + t1]

    def wr_slot(self, i):
        return self.A[:, (R_WR + i * 4096) // 2: (R_WR + (i + 1) * 4096) // 2]

    def d_slot(self, i):
        return self.A[:, (R_WR + i * 2048) // 2: (R_WR + i * 2048) // 2 + 1024]

    def on(self, eng, build, waits=(), inc=None, k=1):
        ws = []
        wd = self.waited[eng]
        for ev in waits:
            if ev is None:
                continue
            s, v = ev
            assert v <= s.n
            if wd.get(id(s), 0) < v:
                wd[id(s)] = v
                ws.append((s, v))
        ev_out = None
        if inc is not None:
            inc.n += k
            ev_out = (inc, inc.n)

        def thunk(e, build=build, ws=tuple(ws), inc=inc, k=k):
            for s, v in ws:
                e.wait_ge(s.h, v)
            i = build(e)
            if inc is not None:
                i.then_inc(inc.h, k)
        self.q[eng].append(thunk)
        return ev_out

    def dma(self, qeng, out, in_, waits=(), sem=None):
        assert sem is not None
        return self.on(qeng, lambda e, o=out, i=in_: e.dma_start(out=o, in_=i),
                       waits, inc=sem, k=16)

    def _rd_evs(self, bank):
        r = self.bank_rd[bank]
        if r is None:
            return []
        return list(r) if isinstance(r, list) else [r]

    def mmg(self, bank, out_ap, parts, waits=(), start=True, stop=True):
        wfirst = list(waits)
        if start:
            wfirst += self._rd_evs(bank)
        n = len(parts)
        ev = None
        for i, (l, r, w) in enumerate(parts):
            allw = (wfirst + list(w)) if i == 0 else list(w)
            ev = self.on("tensor",
                         lambda e, o=out_ap, l=l, r=r, st=(start and i == 0), sp=(stop and i == n - 1):
                         e.matmul(o, l, r, start=st, stop=sp),
                         allw, inc=(self.sem_t if i == n - 1 else None))
        self.bank_wr[bank] = ev
        return ev

    def tpose(self, bank, out_ap, in_ap, ident, waits=()):
        w = list(waits) + self._rd_evs(bank)
        ev = self.on("tensor", lambda e, o=out_ap, i=in_ap, d=ident: e.transpose(o, i, d),
                     w, inc=self.sem_t)
        self.bank_wr[bank] = ev
        return ev

    def bank_read(self, bank, ev):
        self.bank_rd[bank] = ev

    def bcast(self, cols_ap, groups, dsts, bt_pair, bc_pair, rows2, wev):
        u = 0
        evs_out = []
        for gi, grp in enumerate(groups):
            bank = bc_pair[gi % 2]
            evb = None
            for k, cidx in enumerate(grp):
                bt = bt_pair[u % 2]
                row = rows2[u % 2]
                evt = self.tpose(bt, self.PB[bt][0:1, 0:128], cols_ap[:, cidx:cidx + 1],
                                 self.idf, waits=[wev, self.ev_id])
                evv = self.on("vector", lambda e, o=row, i=self.PB[bt][0:1, 0:128]:
                              e.tensor_copy(o, i), [evt], inc=self.sem_v)
                self.bank_read(bt, evv)
                pslice = self.PB[bank][:, k * 128:(k + 1) * 128]
                w0 = self._rd_evs(bank) if k == 0 else []
                evb = self.on("tensor",
                              lambda e, o=pslice, l=self.on1, r=row, st=(k == 0),
                              sp=(k == len(grp) - 1): e.matmul(o, l, r, start=st, stop=sp),
                              w0 + [evv, self.ev_on1],
                              inc=(self.sem_t if k == len(grp) - 1 else None))
                u += 1
            self.bank_wr[bank] = evb
            eve = self.on("scalar", lambda e, o=dsts[gi], i=self.PB[bank][:, 0:128 * len(grp)]:
                          e.copy(o, i), [evb], inc=self.sem_s)
            self.bank_read(bank, eve)
            evs_out.append(eve)
        return evs_out

    def tguard(self):
        return (self.sem_t, self.sem_t.n)

    def vguard(self):
        return (self.sem_v, self.sem_v.n)

    def build(self):
        nc = self.nc
        idf = self.fsl(S_IDF, 128)
        idb = self.bsl(S_IDB, 128)
        on1 = self.fsl(S_ON1, 128, 0, 1)
        onb = self.bsl(S_ONB, 1)
        gam = self.fsl(S_GAM, 16)
        deny = self.fsl(S_DEN, 40)
        mh = self.bsl(S_MH, 128)
        md = self.bsl(S_MD, 128)
        self.idf, self.idb, self.on1, self.onb = idf, idb, on1, onb
        self.gam, self.deny, self.mh, self.md = gam, deny, mh, md

        self.dma("sync", idf, self.d_ident, sem=self.sem_c1)
        self.dma("sync", gam, self.d_gam, sem=self.sem_c1)
        self.dma("sync", deny, self.d_deny, sem=self.sem_c1)
        grp1 = (self.sem_c1, self.sem_c1.n)
        self.ev_id = self.ev_gam = self.ev_deny = grp1
        self.dma("gpsimd", idb, self.d_ident, sem=self.sem_c2)
        self.dma("gpsimd", mh, self.d_mh, sem=self.sem_c2)
        self.dma("gpsimd", md, self.d_md, sem=self.sem_c2)
        grp2 = (self.sem_c2, self.sem_c2.n)
        self.ev_idb = self.ev_mh = self.ev_md = grp2
        self.ev_on1 = self.on("vector", lambda e: e.memset(on1, 1.0), inc=self.sem_v)
        self.ev_onb = self.on("vector", lambda e: e.memset(onb, 1.0), inc=self.sem_v)

        cvals = [EPS_LN, EPS_RMS, -0.5, 1.0 / R, SCALE, 0.5, 1.0 / D]
        ev_cp = None
        for i, v in enumerate(cvals):
            capp = self.fsl(S_CP + i * 32, 1)
            ev_cp = self.on("vector", lambda e, o=capp, vv=v: e.memset(o, vv), inc=self.sem_v)
            self.nc.const_aps.register(F32, v, capp)
        self.on("scalar", lambda e: None, [ev_cp])
        self.on("vector", lambda e: None, [ev_cp])
        self.on("gpsimd", lambda e: None, [ev_cp])

        stage = [self.fsl(ST0, 2048), self.fsl(ST1, 2048)]
        st_free = [None, None]
        ntile = T // 128
        ld_ev = [None] * ntile
        for i in range(2):
            ld_ev[i] = self.dma("sync", stage[i], self.d_x[i * 128:(i + 1) * 128, :],
                                sem=self.sem_x[i])
        pb_in = (6, 7)
        for i in range(ntile):
            for c in range(KC):
                b = pb_in[c % 2]
                evt = self.tpose(b, self.PB[b][:, 0:128], stage[i % 2][:, c * 128:(c + 1) * 128],
                                 idf, waits=[ld_ev[i], self.ev_id])
                if i < 4:
                    dst = self.x_halo(c, i * 128, i * 128 + 128)
                else:
                    dst = self.x_own(c, (i - 4) * 128, (i - 4) * 128 + 128)
                evv = self.on("vector",
                              lambda e, d=dst, p=self.PB[b][:, 0:128]: e.tensor_copy(d, p),
                              [evt], inc=self.sem_v)
                self.bank_read(b, evv)
                last_t = evt
            st_free[i % 2] = last_t
            j = i + 2
            if j < ntile:
                ld_ev[j] = self.dma("sync", stage[j % 2], self.d_x[j * 128:(j + 1) * 128, :],
                                    waits=[st_free[j % 2]], sem=self.sem_x[j % 2])

        self._gates_done = None
        if "ffn1" in self.stages:
            for s in range(3):
                self.ffn_stripe(call=0, stripe=s, cast_guard=self._gates_done)
        if "attn" in self.stages:
            self.attention()
        if "ffn2" in self.stages:
            for s in range(2):
                self.ffn_stripe(call=1, stripe=s,
                                cast_guard=None if s == 0 else self._gates_done)

        gv = self.vguard()
        dma_done = [None, None]
        for i in range(8):
            sl = i % 2
            last_v = None
            for c in range(KC):
                b = pb_in[c % 2]
                evt = self.tpose(b, self.PB[b][:, 0:128],
                                 self.x_own(c, i * 128, i * 128 + 128), idf,
                                 waits=[self.ev_id, gv, dma_done[sl]])
                evv = self.on("vector",
                              lambda e, d=stage[sl][:, c * 128:(c + 1) * 128], p=self.PB[b][:, 0:128]:
                              e.tensor_copy(d, p), [evt], inc=self.sem_v)
                self.bank_read(b, evv)
                last_v = evv
            dma_done[sl] = self.dma("scalar", self.d_y[i * 128:(i + 1) * 128, :], stage[sl],
                                    waits=[last_v], sem=self.sem_y[sl])
        self.on("scalar", lambda e: None, [dma_done[0], dma_done[1]])

        with nc.Block() as blk:
            blk.tensor(lambda e: [f(e) for f in self.q["tensor"]])
            blk.vector(lambda e: [f(e) for f in self.q["vector"]])
            blk.scalar(lambda e: [f(e) for f in self.q["scalar"]])
            blk.gpsimd(lambda e: [f(e) for f in self.q["gpsimd"]])
            blk.sync(lambda e: [f(e) for f in self.q["sync"]])
        return nc

    def ffn_stripe(self, call, stripe, cast_guard=None):
        first = (call == 0 and stripe == 0)
        if call == 0 and stripe == 0:
            xs = [self.x_halo(c) for c in range(KC)]
        elif call == 0:
            t0 = (stripe - 1) * SQ
            xs = [self.x_own(c, t0, t0 + SQ) for c in range(KC)]
        else:
            t0 = stripe * SQ
            xs = [self.x_own(c, t0, t0 + SQ) for c in range(KC)]

        mean_bc = self.fsl(FS_MEAN, 512)
        rstd_bc = self.fsl(FS_RSTD, 512)
        sg2 = [self.fsl(FS_SG[0], 512), self.fsl(FS_SG[1], 512)]
        tmp = self.fsl(FS_TMP, 512)
        cols = self.fsl(FS_COLS, 32)
        rows2 = [self.fsl(FS_ROW0, 128, 0, 1), self.fsl(FS_ROW1, 128, 0, 1)]
        dump2 = [self.fsl(FS_DUMP, 128), self.fsl(FS_DUMP + 512, 128)]
        idf, on1, onb = self.idf, self.on1, self.onb
        gt = self.tguard()
        gs = (self.sem_s, self.sem_s.n)
        cg = cast_guard if cast_guard is not None else gt

        cast_ev = []
        for c in range(KC):
            cast_ev.append(self.on("vector",
                                   lambda e, d=self.xhat(c), s=xs[c]: e.tensor_copy(d, s),
                                   [cg, gs], inc=self.sem_v))
        BXXs, BSUMs = (4, 0), (5, 1)
        stat_evs = []
        for tt in range(4):
            bx = BXXs[tt % 2]
            bs = BSUMs[tt % 2]
            pxx = self.PB[bx][:, 0:128]
            psm = self.PB[bs][:, 0:1]
            parts = [(self.xhat(c)[:, tt * 128:(tt + 1) * 128],
                      self.xhat(c)[:, tt * 128:(tt + 1) * 128], [cast_ev[c]])
                     for c in range(KC)]
            evx = self.mmg(bx, pxx, parts)
            parts = [(self.xhat(c)[:, tt * 128:(tt + 1) * 128], onb, [cast_ev[c], self.ev_onb])
                     for c in range(KC)]
            evs = self.mmg(bs, psm, parts)
            dmp = dump2[tt % 2]
            ev1 = self.on("vector",
                          lambda e, o=dmp, i0=pxx, i1=idf: e.tensor_tensor(o, i0, i1, ALU.mult),
                          [evx, self.ev_id], inc=self.sem_v)
            self.bank_read(bx, ev1)
            ev1 = self.on("vector",
                          lambda e, o=cols[:, 4 + tt:5 + tt], i=dmp:
                          e.reduce_sum(o, i, axis=AX.X), [ev1], inc=self.sem_v)
            ev2 = self.on("vector",
                          lambda e, o=cols[:, tt:tt + 1], i=psm:
                          e.tensor_scalar(o, i, 1.0 / D, None, ALU.mult),
                          [evs], inc=self.sem_v)
            self.bank_read(bs, ev2)
            stat_evs += [ev1, ev2]
        ev = self.on("vector", lambda e, o=dump2[0][:, 0:4], a=cols[:, 0:4]:
                     e.tensor_tensor(o, a, a, ALU.mult), stat_evs, inc=self.sem_v)
        ev = self.on("vector", lambda e, o=cols[:, 4:8], b=dump2[0][:, 0:4]:
                     e.scalar_tensor_tensor(o, o, 1.0 / D, b, ALU.mult, ALU.subtract),
                     [ev], inc=self.sem_v)
        ev = self.on("scalar", lambda e, o=cols[:, 4:8]: e.activation(o, o, AF.Ln, bias=EPS_LN),
                     [ev], inc=self.sem_s)
        ev_rs = self.on("scalar", lambda e, o=cols[:, 4:8]: e.activation(o, o, AF.Exp, scale=-0.5),
                        [ev], inc=self.sem_s)
        ev_bc = self.bcast(cols, [list(range(0, 4)), list(range(4, 8))],
                           [mean_bc, rstd_bc], (2, 3), (4, 5), rows2, ev_rs)
        gt2 = self.tguard()
        app_ev = []
        for c in range(KC):
            e1 = self.on("vector", lambda e, o=tmp, a=xs[c], b=mean_bc: e.tensor_sub(o, a, b),
                         ev_bc + [gt2], inc=self.sem_v)
            e2 = self.on("vector", lambda e, o=self.xhat(c), a=tmp, b=rstd_bc: e.tensor_mul(o, a, b),
                         [e1], inc=self.sem_v)
            app_ev.append(e2)

        ring_loaded = [None] * 4
        ring_free = [gt] * 4
        if first:
            ring_written = [[] for _ in range(4)]
        else:
            ring_written = [[self.ww_last[sl], self.dw_last[2 * sl], self.dw_last[2 * sl + 1]]
                            for sl in range(4)]

        def load_tile(j):
            m = j // 2
            sl = j % 4
            slot = self.wr_slot(sl)
            if first:
                src = self.d_wg[m] if j % 2 == 0 else self.d_wu[m]
                ev = self.dma("gpsimd", slot, src, waits=[ring_free[sl]] + ring_written[sl],
                              sem=self.sem_ws[sl])
                ring_loaded[sl] = ev
                dst = self.d_wgs[m] if j % 2 == 0 else self.d_wus[m]
                evw = self.dma("scalar", dst, slot, waits=[ev], sem=self.sem_ww[sl])
                ring_written[sl] = [evw]
                self.ww_last[sl] = evw
            else:
                ring_loaded[sl] = self.dma("sync", slot, src_scr(j),
                                           waits=[ring_free[sl]] + ring_written[sl],
                                           sem=self.sem_ws[sl])
                ring_written[sl] = []

        def src_scr(j):
            m = j // 2
            return self.d_wgs[m] if j % 2 == 0 else self.d_wus[m]

        for j in range(4):
            load_tile(j)
        BG = [0, 2]
        BU = [1, 3]
        hm_ev = [None] * FFC
        for m in range(FFC):
            bg = BG[m % 2]
            bu = BU[m % 2]
            slg = (2 * m) % 4
            slu = (2 * m + 1) % 4
            pg = self.PB[bg][:, 0:512]
            pu = self.PB[bu][:, 0:512]
            parts = [(self.wr_slot(slg)[:, k * 128:(k + 1) * 128], self.xhat(k),
                      [ring_loaded[slg], app_ev[k]]) for k in range(KC)]
            evg = self.mmg(bg, pg, parts)
            parts = [(self.wr_slot(slu)[:, k * 128:(k + 1) * 128], self.xhat(k),
                      [ring_loaded[slu], app_ev[k]]) for k in range(KC)]
            evu = self.mmg(bu, pu, parts)
            ring_free[slg] = evg
            ring_free[slu] = evu
            sg = sg2[m % 2]
            evsi = self.on("scalar", lambda e, o=sg, i=pg: e.activation(o, i, AF.Silu),
                           [evg] + ([hm_ev[m - 2]] if m >= 2 else []), inc=self.sem_s)
            self.bank_read(bg, evsi)
            evh = self.on("vector", lambda e, o=self.h_ffn(m), a=sg, b=pu: e.tensor_mul(o, a, b),
                          [evsi, evu], inc=self.sem_v)
            self.bank_read(bu, evh)
            hm_ev[m] = evh
            for jj in (2 * m + 4, 2 * m + 5):
                if 4 <= jj < 2 * FFC:
                    load_tile(jj)

        if self.dbg and call == 0 and stripe == 1:
            self._dbg_cap = []

        def dbg_cap(idx, src, wv):
            dst = self.A32[:, (R_XH + idx * 2048) // 4: (R_XH + idx * 2048) // 4 + 512]
            cv = self.on("vector", lambda e, o=dst, s=src: e.tensor_copy(o, s),
                         [wv], inc=self.sem_v)
            self._dbg_cap.append(self.dma("scalar", self.d_dbg[:, idx * 512:(idx + 1) * 512],
                                          dst, waits=[cv], sem=self.sem_dbg))

        sp_ev = [None] * FFC
        for s16 in range(16):
            hm = [self.h_ffn(4 * s16 + j) for j in range(4)]
            wev = [hm_ev[4 * s16 + j] for j in range(4)]
            a = [self.bsl(SP_A[j], 512) for j in range(4)]
            w = [self.bsl(SP_W[j], 512) for j in range(3)]
            aev = []
            for j in range(4):
                prevw = [sp_ev[4 * (s16 - 1) + j]] if s16 > 0 else []
                aev.append(self.on("scalar", lambda e, o=a[j], i=hm[j]: e.activation(o, i, AF.Abs),
                                   [wev[j]] + prevw, inc=self.sem_s))
            e1 = self.on("vector", lambda e, o=w[0], x=a[0], y=a[1]: e.tensor_tensor(o, x, y, ALU.max),
                         [aev[0], aev[1]], inc=self.sem_v)
            e2 = self.on("vector", lambda e, o=w[1], x=a[2], y=a[3]: e.tensor_tensor(o, x, y, ALU.max),
                         [aev[2], aev[3]], inc=self.sem_v)
            e3 = self.on("vector", lambda e, o=w[0], x=w[0], y=w[1]: e.tensor_tensor(o, x, y, ALU.min),
                         [e1, e2], inc=self.sem_v)
            e4 = self.on("vector", lambda e, o=w[1], x=a[0], y=a[1]: e.tensor_tensor(o, x, y, ALU.min),
                         [e3], inc=self.sem_v)
            e5 = self.on("vector", lambda e, o=w[2], x=a[2], y=a[3]: e.tensor_tensor(o, x, y, ALU.min),
                         [e4], inc=self.sem_v)
            e6 = self.on("vector", lambda e, o=w[1], x=w[1], y=w[2]: e.tensor_tensor(o, x, y, ALU.max),
                         [e5], inc=self.sem_v)
            e_t = self.on("vector", lambda e, o=w[0], x=w[0], y=w[1]: e.tensor_tensor(o, x, y, ALU.max),
                          [e6], inc=self.sem_v)
            if self.dbg and call == 0 and stripe == 1 and s16 == 0:
                dbg_cap(0, hm[0], wev[0])
                dbg_cap(1, a[0], aev[0])
                dbg_cap(2, a[1], aev[1])
                dbg_cap(3, w[0], e_t)
            for j in range(4):
                e_msk = self.on("vector", lambda e, aj=a[j], t=w[0]: e.tensor_tensor(aj, aj, t, ALU.is_ge),
                                [e_t, aev[j]], inc=self.sem_v)
                if self.dbg and call == 0 and stripe == 1 and s16 == 0 and j == 0:
                    dbg_cap(4, a[0], e_msk)
                sp_ev[4 * s16 + j] = self.on("vector",
                                             lambda e, hj=hm[j], aj=a[j]: e.tensor_mul(hj, hj, aj),
                                             [e_msk], inc=self.sem_v)
                if self.dbg and call == 0 and stripe == 1 and s16 == 0 and j == 0:
                    dbg_cap(5, hm[0], sp_ev[0])

        gt3 = self.tguard()
        self._gates_done = gt3
        dslot_loaded = [None] * 8
        dslot_free = [gt3] * 8
        if first:
            dslot_written = [[self.ww_last[i // 2]] for i in range(8)]
        else:
            dslot_written = [[self.dw_last[i], self.ww_last[i // 2]] for i in range(8)]

        def load_dgroup(dc, g):
            i = (dc * 8 + g) % 8
            if first:
                ev = self.dma("gpsimd", self.d_slot(i), self.d_wd[dc][:, g * 1024:(g + 1) * 1024],
                              waits=[dslot_free[i]] + dslot_written[i], sem=self.sem_ds[i])
                dslot_loaded[i] = ev
                evw = self.dma("scalar", self.d_wds[dc][:, g * 1024:(g + 1) * 1024],
                               self.d_slot(i), waits=[ev], sem=self.sem_dw[i])
                dslot_written[i] = [evw]
                self.dw_last[i] = evw
            else:
                dslot_loaded[i] = self.dma("sync", self.d_slot(i),
                                           self.d_wds[dc][:, g * 1024:(g + 1) * 1024],
                                           waits=[dslot_free[i]] + dslot_written[i],
                                           sem=self.sem_ds[i])
                dslot_written[i] = []

        for g in range(8):
            load_dgroup(0, g)
        BD = [6, 7]
        for dc in range(KC):
            bd = BD[dc % 2]
            pd = self.PB[bd][:, 0:512]
            ev = None
            for g in range(8):
                i = (dc * 8 + g) % 8
                parts = [(self.d_slot(i)[:, ii * 128:(ii + 1) * 128], self.h_ffn(g * 8 + ii),
                          [dslot_loaded[i], sp_ev[g * 8 + ii]]) for ii in range(8)]
                ev = self.mmg(bd, pd, parts, start=(g == 0), stop=(g == 7))
                dslot_free[i] = ev
                if dc + 1 < KC:
                    load_dgroup(dc + 1, g)
            if self.dbg and call == 0 and stripe == 1 and dc == 0:
                dbg_cap(6, pd, ev)
                dbg_cap(8, self.d_slot(0)[:, 0:512], dslot_loaded[0])
                dbg_cap(9, self.h_ffn(32), sp_ev[32])
                dbg_cap(10, self.h_ffn(8), sp_ev[8])
            eve = self.on("vector",
                          lambda e, o=xs[dc], p=pd: e.scalar_tensor_tensor(o, p, 0.5, o, ALU.mult, ALU.add),
                          [ev], inc=self.sem_v)
            self.bank_read(bd, eve)
            if self.dbg and call == 0 and stripe == 1 and dc == 0:
                dbg_cap(7, xs[0], eve)

    def attention(self):
        idf, idb, on1, onb = self.idf, self.idb, self.on1, self.onb
        gam, deny, mh, md = self.gam, self.deny, self.mh, self.md
        mean_bc = self.fsl(L2_MEAN, T)
        rstd_bc = self.fsl(L2_RSTD, T)
        tmp = self.fsl(L2_TMP, T)
        cols24 = self.fsl(AT_COLS, 32)
        rows2 = [self.fsl(AT_ROW0, 128, 0, 1), self.fsl(AT_ROW1, 128, 0, 1)]
        dump2 = [self.fsl(AT_DUMP0, 128), self.fsl(AT_DUMP1, 128)]
        gt = self.tguard()
        gs = (self.sem_s, self.sem_s.n)
        cg = gt

        cast_ev = []
        for c in range(KC):
            e1 = self.on("vector", lambda e, o=self.hv_c(c, 0, HALO), s=self.x_halo(c):
                         e.tensor_copy(o, s), [cg, gs], inc=self.sem_v)
            e2 = self.on("vector", lambda e, o=self.hv_c(c, HALO, T), s=self.x_own(c):
                         e.tensor_copy(o, s), [e1], inc=self.sem_v)
            cast_ev.append(e2)

        BXXs, BSUMs = (4, 0), (5, 1)
        stat_evs = []
        for tt in range(12):
            bx = BXXs[tt % 2]
            bs = BSUMs[tt % 2]
            pxx = self.PB[bx][:, 0:128]
            psm = self.PB[bs][:, 0:1]
            parts = [(self.hv_c(c, tt * 128, tt * 128 + 128),
                      self.hv_c(c, tt * 128, tt * 128 + 128), [cast_ev[c]]) for c in range(KC)]
            evx = self.mmg(bx, pxx, parts)
            parts = [(self.hv_c(c, tt * 128, tt * 128 + 128), onb, [cast_ev[c], self.ev_onb])
                     for c in range(KC)]
            evs = self.mmg(bs, psm, parts)
            dmp = dump2[tt % 2]
            ev1 = self.on("vector",
                          lambda e, o=dmp, i0=pxx, i1=idf: e.tensor_tensor(o, i0, i1, ALU.mult),
                          [evx, self.ev_id], inc=self.sem_v)
            self.bank_read(bx, ev1)
            ev1 = self.on("vector",
                          lambda e, o=cols24[:, 12 + tt:13 + tt], i=dmp:
                          e.reduce_sum(o, i, axis=AX.X), [ev1], inc=self.sem_v)
            ev2 = self.on("vector",
                          lambda e, o=cols24[:, tt:tt + 1], i=psm:
                          e.tensor_scalar(o, i, 1.0 / D, None, ALU.mult),
                          [evs], inc=self.sem_v)
            self.bank_read(bs, ev2)
            stat_evs += [ev1, ev2]
        ev = self.on("vector", lambda e, o=dump2[0][:, 0:12], a=cols24[:, 0:12]:
                     e.tensor_tensor(o, a, a, ALU.mult), stat_evs, inc=self.sem_v)
        ev = self.on("vector", lambda e, o=cols24[:, 12:24], b=dump2[0][:, 0:12]:
                     e.scalar_tensor_tensor(o, o, 1.0 / D, b, ALU.mult, ALU.subtract),
                     [ev], inc=self.sem_v)
        ev = self.on("scalar", lambda e, o=cols24[:, 12:24]: e.activation(o, o, AF.Ln, bias=EPS_LN),
                     [ev], inc=self.sem_s)
        ev_rs = self.on("scalar", lambda e, o=cols24[:, 12:24]: e.activation(o, o, AF.Exp, scale=-0.5),
                        [ev], inc=self.sem_s)

        segs_m = [mean_bc[:, s * 512:(s + 1) * 512] for s in range(3)]
        segs_r = [rstd_bc[:, s * 512:(s + 1) * 512] for s in range(3)]
        ev_mb = self.bcast(cols24, [[0, 1, 2, 3], [4, 5, 6, 7], [8, 9, 10, 11]],
                           segs_m, (6, 7), (2, 3), rows2, ev_rs)
        ev_rb = self.bcast(cols24, [[12, 13, 14, 15], [16, 17, 18, 19], [20, 21, 22, 23]],
                           segs_r, (6, 7), (2, 3), rows2, ev_rs)
        gt2 = self.tguard()

        passa_ev = []
        for c in range(KC):
            e1 = self.on("vector", lambda e, o=tmp[:, 0:HALO], a=self.x_halo(c), b=mean_bc[:, 0:HALO]:
                         e.tensor_sub(o, a, b), ev_mb + [gt2], inc=self.sem_v)
            e2 = self.on("vector", lambda e, o=tmp[:, HALO:T], a=self.x_own(c), b=mean_bc[:, HALO:T]:
                         e.tensor_sub(o, a, b), [e1], inc=self.sem_v)
            e3 = self.on("vector", lambda e, o=tmp, b=rstd_bc: e.tensor_mul(o, o, b),
                         [e2] + ev_rb, inc=self.sem_v)
            e4 = self.on("vector", lambda e, o=self.hv_c(c), s=tmp: e.tensor_copy(o, s),
                         [e3], inc=self.sem_v)
            e5 = self.on("vector", lambda e, o=self.x_own(c), b=tmp[:, HALO:T]:
                         e.tensor_add(o, o, b), [e4], inc=self.sem_v)
            passa_ev.append(e5)

        rms_evs = []
        for tt in range(12):
            bx = BXXs[tt % 2]
            pxx = self.PB[bx][:, 0:128]
            parts = [(self.hv_c(c, tt * 128, tt * 128 + 128),
                      self.hv_c(c, tt * 128, tt * 128 + 128), [passa_ev[c]]) for c in range(KC)]
            evx = self.mmg(bx, pxx, parts)
            dmp = dump2[tt % 2]
            ev1 = self.on("vector",
                          lambda e, o=dmp, i0=pxx, i1=idf: e.tensor_tensor(o, i0, i1, ALU.mult),
                          [evx, self.ev_id], inc=self.sem_v)
            self.bank_read(bx, ev1)
            ev1 = self.on("vector",
                          lambda e, o=cols24[:, tt:tt + 1], i=dmp:
                          e.reduce_sum(o, i, axis=AX.X), [ev1], inc=self.sem_v)
            rms_evs.append(ev1)
        ev = self.on("scalar", lambda e, o=cols24[:, 0:12]:
                     e.activation(o, o, AF.Ln, bias=EPS_RMS, scale=1.0 / D),
                     rms_evs, inc=self.sem_s)
        ev_r = self.on("scalar", lambda e, o=cols24[:, 0:12]: e.activation(o, o, AF.Exp, scale=-0.5),
                       [ev], inc=self.sem_s)
        ev_rbc = self.bcast(cols24, [[0, 1, 2, 3], [4, 5, 6, 7], [8, 9, 10, 11]],
                            segs_r, (6, 7), (2, 3), rows2, ev_r)

        passb_ev = []
        for c in range(KC):
            passb_ev.append(self.on("vector",
                                    lambda e, o=self.hv_c(c), g=gam[:, c:c + 1], rb=rstd_bc:
                                    e.scalar_tensor_tensor(o, o, g, rb, ALU.mult, ALU.mult),
                                    ev_rbc + [self.ev_gam], inc=self.sem_v))

        ga = self.vguard()
        cosb = self.bsl(A_CS, T)
        sinb = self.bsl(A_SN, T)
        self.dma("gpsimd", cosb, self.d_cos, waits=[ga], sem=self.sem_cs)
        self.dma("gpsimd", sinb, self.d_sin, waits=[ga], sem=self.sem_cs)
        ev_cos = ev_sin = (self.sem_cs, self.sem_cs.n)

        qa_sb = self.bsl(AT_QA, 256)
        ka_sb = self.bsl(AT_QA + 1024, 256)
        va_sb = self.bsl(AT_QA + 2048, 256)
        self.dma("gpsimd", qa_sb, self.d_qa, waits=[gt], sem=self.sem_la)
        self.dma("gpsimd", ka_sb, self.d_ka, waits=[gt], sem=self.sem_la)
        self.dma("gpsimd", va_sb, self.d_va, waits=[gt], sem=self.sem_la)
        ev_qa = ev_ka = ev_va = (self.sem_la, self.sem_la.n)
        uq = self.bsl(FS_UQ, T, 0, 16)
        uk = self.bsl(FS_UK, T, 0, 16)
        uv = self.bsl(FS_UV, T, 0, 16)
        BL = 5
        u_ev = {}
        for nm, asb, aev, udst in (("q", qa_sb, ev_qa, uq), ("k", ka_sb, ev_ka, uk),
                                   ("v", va_sb, ev_va, uv)):
            eve = None
            for seg in range(3):
                pl = self.PB[BL][0:16, 0:512]
                parts = [(asb[:, k * 16:(k + 1) * 16], self.hv_c(k, seg * 512, (seg + 1) * 512),
                          [aev, passb_ev[k]]) for k in range(KC)]
                ev = self.mmg(BL, pl, parts)
                eve = self.on("scalar",
                              lambda e, o=udst[:, seg * 512:(seg + 1) * 512], i=pl:
                              e.activation(o, i, AF.Copy, scale=1.0 / R), [ev], inc=self.sem_s)
                self.bank_read(BL, eve)
            u_ev[nm] = eve

        lb_buf = {}
        lb_ev = {}
        for i, nm in enumerate(("q", "k", "v")):
            lb_buf[nm] = [self.bsl(A_LB + (2 * i + j) * 512, 128, 0, 16) for j in range(2)]
            lb_ev[nm] = [None, None]
        d_lb = {"q": self.d_qb, "k": self.d_kb, "v": self.d_vb}

        def load_lb(nm, h, consumed):
            sl = h % 2
            lb_ev[nm][sl] = self.dma("gpsimd", lb_buf[nm][sl], d_lb[nm][:, h * 128:(h + 1) * 128],
                                     waits=[consumed, ga], sem=self.sem_lb[(nm, sl)])
        for nm in ("q", "k", "v"):
            load_lb(nm, 0, None)
            load_lb(nm, 1, None)

        order = [(w, h) for h in range(H) for w in ("q", "k", "v")]
        wring_loaded = {}
        wring_free = [self.tguard()] * 4
        wring_written = [[self.ww_last[sl], self.dw_last[2 * sl], self.dw_last[2 * sl + 1]]
                         for sl in range(4)]
        self._widx = 0

        def pump():
            if self._widx >= len(order):
                return
            which, h = order[self._widx]
            sl = self._widx % 4
            src = {"q": self.d_wq, "k": self.d_wk, "v": self.d_wv}[which][h]
            ev = self.dma("gpsimd", self.wr_slot(sl), src,
                          waits=[wring_free[sl]] + wring_written[sl], sem=self.sem_ws[sl])
            wring_written[sl] = []
            wring_loaded[(which, h)] = (sl, ev)
            self._widx += 1

        for _ in range(4):
            pump()

        t1 = self.fsl(A_T1, 512)
        t2 = self.fsl(A_T2, 512)
        scr2 = [self.fsl(A_SC, 640), self.fsl(A_SC2, 640), self.fsl(A_SC3, 640)]
        kbf = self.bsl(A_KBF, T)
        vf = self.bsl(A_VF, T)
        pbf2 = [self.bsl(A_P, 640), self.bsl(A_P2, 640), self.bsl(A_P3, 640)]
        pt = [self.bsl(A_PT, 128), self.bsl(A_PT + 256, 128)]

        def nm_ap(p, idx):
            off = (A_NM + p * 16 + idx * 4) if p < 2 else (A_NM3 + idx * 4)
            return self.A32[:, off // 4: off // 4 + 1]

        def rope(psums, cos_ap, sin_ap, out_ap, mm_evs):
            evs = []
            for i, (bk, pp) in enumerate(psums):
                s0 = i * 512
                e1 = self.on("vector", lambda e, o=t1, a=pp, b=cos_ap[:, s0:s0 + 512]:
                             e.tensor_mul(o, a, b), [mm_evs[i], ev_cos], inc=self.sem_v)
                e2 = self.on("vector",
                             lambda e, o=t2[0:64, :], a=pp[64:128, :], b=sin_ap[0:64, s0:s0 + 512]:
                             e.tensor_mul(o, a, b), [mm_evs[i], ev_sin], inc=self.sem_v)
                e3 = self.on("vector",
                             lambda e, o=t2[64:128, :], a=pp[0:64, :], b=sin_ap[64:128, s0:s0 + 512]:
                             e.tensor_mul(o, a, b), [mm_evs[i], ev_sin], inc=self.sem_v)
                self.bank_read(bk, e3)
                e4 = self.on("vector",
                             lambda e, o=out_ap[0:64, s0:s0 + 512], a=t1[0:64, :], b=t2[0:64, :]:
                             e.tensor_sub(o, a, b), [e1, e2], inc=self.sem_v)
                e5 = self.on("vector",
                             lambda e, o=out_ap[64:128, s0:s0 + 512], a=t1[64:128, :], b=t2[64:128, :]:
                             e.tensor_add(o, a, b), [e3, e4], inc=self.sem_v)
                evs.append(e5)
            return evs

        cos_own = self.bsl(A_CS + 2 * HALO, TOK)
        sin_own = self.bsl(A_SN + 2 * HALO, TOK)

        for h in range(H):
            slq, evq = wring_loaded.pop(("q", h))
            slk, evk = wring_loaded.pop(("k", h))
            slv, evv_ = wring_loaded.pop(("v", h))
            q_banks = [(0, self.PB[0][:, 0:512]), (1, self.PB[1][:, 0:512])]
            q_mm = []
            for i, (bk, pp) in enumerate(q_banks):
                parts = [(self.wr_slot(slq)[:, k * 128:(k + 1) * 128],
                          self.hv_c(k, HALO + i * 512, HALO + (i + 1) * 512),
                          [evq, passb_ev[k]]) for k in range(KC)]
                parts.append((lb_buf["q"][h % 2], uq[:, HALO + i * 512:HALO + (i + 1) * 512],
                              [lb_ev["q"][h % 2], u_ev["q"]]))
                q_mm.append(self.mmg(bk, pp, parts))
            wring_free[slq] = q_mm[-1]
            rope_q = rope(q_banks, cos_own, sin_own, self.qa_sl(h), q_mm)
            k_banks = [(2, self.PB[2][:, 0:512]), (3, self.PB[3][:, 0:512]), (0, self.PB[0][:, 0:512])]
            k_mm = []
            for i, (bk, pp) in enumerate(k_banks):
                parts = [(self.wr_slot(slk)[:, k * 128:(k + 1) * 128],
                          self.hv_c(k, i * 512, (i + 1) * 512),
                          [evk, passb_ev[k]]) for k in range(KC)]
                parts.append((lb_buf["k"][h % 2], uk[:, i * 512:(i + 1) * 512],
                              [lb_ev["k"][h % 2], u_ev["k"]]))
                k_mm.append(self.mmg(bk, pp, parts))
            wring_free[slk] = k_mm[-1]
            rope_k = rope(k_banks, self.bsl(A_CS, T), self.bsl(A_SN, T), kbf, k_mm)
            v_banks = [(1, self.PB[1][:, 0:512]), (2, self.PB[2][:, 0:512]), (3, self.PB[3][:, 0:512])]
            v_ev = []
            for i, (bk, pp) in enumerate(v_banks):
                parts = [(self.wr_slot(slv)[:, k * 128:(k + 1) * 128],
                          self.hv_c(k, i * 512, (i + 1) * 512),
                          [evv_, passb_ev[k]]) for k in range(KC)]
                parts.append((lb_buf["v"][h % 2], uv[:, i * 512:(i + 1) * 512],
                              [lb_ev["v"][h % 2], u_ev["v"]]))
                ev = self.mmg(bk, pp, parts)
                eve = self.on("vector", lambda e, o=vf[:, i * 512:(i + 1) * 512], p=pp: e.tensor_copy(o, p),
                              [ev], inc=self.sem_v)
                self.bank_read(bk, eve)
                v_ev.append(eve)
            wring_free[slv] = v_ev[-1]
            if h + 2 < H:
                for nm in ("q", "k", "v"):
                    load_lb(nm, h + 2, {"q": q_mm[-1], "k": k_mm[-1], "v": v_ev[-1]}[nm])
            for _ in range(3):
                pump()
            BT2 = [4, 1]
            vt_ev = [None] * 12
            for jt in range(12):
                bk = BT2[jt % 2]
                evt = self.tpose(bk, self.PBb[bk][:, 0:128], vf[:, jt * 128:(jt + 1) * 128], idb,
                                 waits=[v_ev[jt // 4], self.ev_idb])
                eve = self.on("scalar", lambda e, o=vf[:, jt * 128:(jt + 1) * 128], i=self.PBb[bk][:, 0:128]:
                              e.copy(o, i), [evt], inc=self.sem_s)
                self.bank_read(bk, eve)
                vt_ev[jt] = eve

            SC_BANKS = [(6, 7), (2, 3)]
            TP_BANKS = (4, 1)
            b_av = 5
            sm_ev = [None] * 8

            def emit_scores(qt):
                b_s1, b_s2 = SC_BANKS[qt % 2]
                ps1 = self.PB[b_s1][:, 0:512]
                ps2 = self.PB[b_s2][:, 0:128]
                ev_s1 = self.mmg(b_s1, ps1,
                                 [(self.qa_sl(h, qt * 128, qt * 128 + 128),
                                   kbf[:, qt * 128: qt * 128 + 512],
                                   [rope_q[-1], rope_k[-1]])])
                ev_s2 = self.mmg(b_s2, ps2,
                                 [(self.qa_sl(h, qt * 128, qt * 128 + 128),
                                   kbf[:, qt * 128 + 512: qt * 128 + 640],
                                   [rope_k[-1]])])
                return ev_s1, ev_s2

            def emit_softmax(qt, ev_s1, ev_s2):
                b_s1, b_s2 = SC_BANKS[qt % 2]
                p = qt % 3
                ps1 = self.PB[b_s1][:, 0:512]
                ps2 = self.PB[b_s2][:, 0:128]
                scr = scr2[p]
                pbf = pbf2[p]
                nmax, nm2 = nm_ap(p, 0), nm_ap(p, 1)
                rows, rinv = nm_ap(p, 2), nm_ap(p, 3)
                mev = []
                for b in range(5):
                    src = ps1[:, b * 128:(b + 1) * 128] if b < 4 else ps2
                    dcol = deny[:, qt * 5 + b: qt * 5 + b + 1]
                    dst = scr[:, b * 128:(b + 1) * 128]
                    wv = [ev_s1 if b < 4 else ev_s2, self.ev_deny]
                    if b == 0:
                        ev = self.on("vector", lambda e, o=dst, s=src, d=dcol, m=mh:
                                     e.scalar_tensor_tensor(o, s, d, m, ALU.add, ALU.add),
                                     wv + [self.ev_mh], inc=self.sem_v)
                    elif b == 4:
                        ev = self.on("vector", lambda e, o=dst, s=src, d=dcol, m=md:
                                     e.scalar_tensor_tensor(o, s, d, m, ALU.add, ALU.add),
                                     wv + [self.ev_md], inc=self.sem_v)
                    elif _SC_EVAC:
                        ev = self.on("scalar", lambda e, o=dst, s=src, d=dcol:
                                     e.activation(o, s, AF.Identity, bias=d[:, 0:1]),
                                     wv, inc=self.sem_s)
                    else:
                        ev = self.on("vector", lambda e, o=dst, s=src, d=dcol:
                                     e.tensor_scalar(o, s, d, None, ALU.add),
                                     wv, inc=self.sem_v)
                    mev.append(ev)
                self.bank_read(b_s1, [mev[1], mev[2], mev[3], mev[0]])
                self.bank_read(b_s2, mev[4])
                ev = self.on("vector", lambda e, o=nmax, s=scr: e.reduce_max(o, s, axis=AX.X, negate=True),
                             mev, inc=self.sem_v)
                ev = self.on("vector", lambda e, o=nm2, i=nmax: e.tensor_scalar(o, i, SCALE, None, ALU.mult),
                             [ev], inc=self.sem_v)
                ev = self.on("scalar", lambda e, o=pbf, s=scr, b2=nm2, r=rows:
                             e.activation(o, s, AF.Exp, bias=b2[:, 0:1], scale=SCALE, accum_out=r),
                             [ev], inc=self.sem_s)
                ev = self.on("vector", lambda e, o=rinv, i=rows: e.reciprocal(o, i), [ev], inc=self.sem_v)
                sm_ev[qt] = self.on("vector", lambda e, o=pbf, r=rinv:
                                    e.tensor_scalar(o, o, r[:, 0:1], None, ALU.mult),
                                    [ev], inc=self.sem_v)

            def emit_consume(qt):
                pbf = pbf2[qt % 3]
                pav = self.PB[b_av][:, 0:128]
                ev_av = None
                for b in range(5):
                    bk = TP_BANKS[b % 2]
                    evt = self.tpose(bk, self.PBb[bk][:, 0:128], pbf[:, b * 128:(b + 1) * 128], idb,
                                     waits=[sm_ev[qt], self.ev_idb])
                    eve = self.on("scalar", lambda e, o=pt[b % 2], i=self.PBb[bk][:, 0:128]:
                                  e.copy(o, i), [evt], inc=self.sem_s)
                    self.bank_read(bk, eve)
                    w0 = self._rd_evs(b_av) if b == 0 else []
                    ev_av = self.on("tensor",
                                    lambda e, o=pav, l=vf[:, (qt + b) * 128:(qt + b + 1) * 128],
                                    r=pt[b % 2], st=(b == 0), sp=(b == 4):
                                    e.matmul(o, l, r, start=st, stop=sp),
                                    w0 + [eve, vt_ev[qt + b]], inc=self.sem_t)
                self.bank_wr[b_av] = ev_av
                eva = self.on("scalar",
                              lambda e, o=self.qa_sl(h, qt * 128, qt * 128 + 128), i=pav:
                              e.copy(o, i), [ev_av], inc=self.sem_s)
                self.bank_read(b_av, eva)

            if False and self.dbg2 and h == 0:
                self._acap_last = None

                def acap(idx, src, wv):
                    dst = self.A32[:, (R_H + 28672) // 4: (R_H + 28672) // 4 + 512]
                    cv = self.on("vector", lambda e, o=dst, s=src: e.tensor_copy(o, s),
                                 [wv, self._acap_last], inc=self.sem_v)
                    self._acap_last = self.dma("scalar", self.d_dbg[:, idx * 512:(idx + 1) * 512],
                                               dst, waits=[cv], sem=self.sem_dbg)
                acap(0, self.hv_c(0, 0, 512), passb_ev[0])
                acap(1, kbf[:, 0:512], rope_k[-1])
                acap(2, vf[:, 0:512], vt_ev[3])
                acap(3, self.qa_sl(0, 0, 512), rope_q[-1])
                acap(4, self.bsl(A_CS, 512), ev_cos)
                acap(5, self.bsl(A_SN, 512), ev_sin)
                acap(6, mean_bc[:, 0:512], ev_mb[0])
                acap(7, rstd_bc[:, 0:512], ev_rbc[0])

            if _ATTN_LITE:
                pass
            elif _QT_PIPE:
                for qt in range(8):
                    es = emit_scores(qt)
                    if qt >= 2:
                        emit_consume(qt - 2)
                    emit_softmax(qt, *es)
                emit_consume(6)
                emit_consume(7)
            else:
                for qt in range(8):
                    es = emit_scores(qt)
                    emit_softmax(qt, *es)
                    emit_consume(qt)

        attn_done = self.vguard()
        attn_done_s = (self.sem_s, self.sem_s.n)
        oa_sb = self.bsl(O_OA, 256)
        ob_buf = [self.bsl(O_OB + j * 512, 128, 0, 16) for j in range(2)]
        uo = self.bsl(O_UO, TOK, 0, 16)
        ev_oa = self.dma("gpsimd", oa_sb, self.d_oa, waits=[attn_done], sem=self.sem_oa)
        ev_uo = None
        for seg in range(2):
            pl = self.PB[BL][0:16, 0:512]
            parts = [(oa_sb[:, k * 16:(k + 1) * 16], self.qa_sl(k, seg * 512, (seg + 1) * 512),
                      [ev_oa, attn_done, attn_done_s]) for k in range(KC)]
            ev = self.mmg(BL, pl, parts)
            ev_uo = self.on("scalar", lambda e, o=uo[:, seg * 512:(seg + 1) * 512], i=pl:
                            e.activation(o, i, AF.Copy, scale=1.0 / R), [ev], inc=self.sem_s)
            self.bank_read(BL, ev_uo)
        oload = [None] * KC
        ob_ev = [None] * KC
        ob_consumed = [None] * KC

        def load_o(m, wait_ev):
            sl = m % 4
            oload[m] = self.dma("gpsimd", self.wr_slot(sl), self.d_wo[m],
                                waits=[wring_free[sl] if m < 4 else wait_ev],
                                sem=self.sem_ws[sl])
            ob_ev[m] = self.dma("gpsimd", ob_buf[m % 2], self.d_ob[:, m * 128:(m + 1) * 128],
                                waits=[attn_done if m < 2 else ob_consumed[m - 2]],
                                sem=self.sem_ob[m % 2])

        for m in range(2):
            load_o(m, None)
        BO = [0, 1]
        for m in range(KC):
            bo = BO[m % 2]
            po = self.PB[bo][:, 0:512]
            ev = None
            for seg in range(2):
                parts = [(self.wr_slot(m % 4)[:, k * 128:(k + 1) * 128],
                          self.qa_sl(k, seg * 512, (seg + 1) * 512),
                          [oload[m], attn_done, attn_done_s]) for k in range(KC)]
                parts.append((ob_buf[m % 2], uo[:, seg * 512:(seg + 1) * 512],
                              [ob_ev[m], ev_uo]))
                ev = self.mmg(bo, po, parts)
                eve = self.on("vector",
                              lambda e, o=self.x_own(m, seg * 512, (seg + 1) * 512), p=po:
                              e.tensor_tensor(o, o, p, ALU.add), [ev], inc=self.sem_v)
                self.bank_read(bo, eve)
            ob_consumed[m] = ev
            if m + 2 < KC:
                load_o(m + 2, ev if m + 2 >= 4 else None)


_CACHE = {}


def _perm():
    s = np.arange(16)[:, None, None]
    j = np.arange(4)[None, :, None]
    g = np.arange(128)[None, None, :]
    new = (512 * s + 128 * j + g).ravel()
    old = (512 * s + 4 * g + j).ravel()
    p = np.empty(FF, dtype=np.int64)
    p[new] = old
    return p


def _host_weights(inp):
    perm = _perm()
    wg = np.ascontiguousarray(np.asarray(inp["L0_ffn_w_gate"], np.float32))[:, perm]
    wu = np.ascontiguousarray(np.asarray(inp["L0_ffn_w_up"], np.float32))[:, perm]
    wd = np.ascontiguousarray(np.asarray(inp["L0_ffn_w_down"], np.float32))[perm, :]
    wg_h = np.ascontiguousarray(wg.reshape(16, 128, 64, 128).transpose(2, 1, 0, 3).reshape(64, 128, 2048))
    wu_h = np.ascontiguousarray(wu.reshape(16, 128, 64, 128).transpose(2, 1, 0, 3).reshape(64, 128, 2048))
    wd_h = np.ascontiguousarray(wd.reshape(64, 128, 16, 128).transpose(2, 1, 0, 3).reshape(16, 128, 8192))

    def projT(w):
        wt = np.ascontiguousarray(np.asarray(w, np.float32).T)
        return np.ascontiguousarray(wt.reshape(16, 128, 16, 128).transpose(2, 1, 0, 3).reshape(16, 128, 2048))

    def loraA(a):
        return np.ascontiguousarray(np.asarray(a, np.float32).T.reshape(16, 128, 16)
                                    .transpose(1, 0, 2).reshape(128, 256))

    return dict(
        wg=wg_h, wu=wu_h, wd=wd_h,
        wq=projT(inp["L0_attn_wq"]), wk=projT(inp["L0_attn_wk"]),
        wv=projT(inp["L0_attn_wv"]), wo=projT(inp["L0_attn_wo"]),
        qa=loraA(inp["L0_attn_qA"]), ka=loraA(inp["L0_attn_kA"]),
        va=loraA(inp["L0_attn_vA"]), oa=loraA(inp["L0_attn_oA"]),
        qb=np.ascontiguousarray(np.asarray(inp["L0_attn_qB"], np.float32).T),
        kb=np.ascontiguousarray(np.asarray(inp["L0_attn_kB"], np.float32).T),
        vb=np.ascontiguousarray(np.asarray(inp["L0_attn_vB"], np.float32).T),
        ob=np.ascontiguousarray(np.asarray(inp["L0_attn_oB"], np.float32).T),
        gam=np.ascontiguousarray(np.asarray(inp["L0_attn_norm_w"], np.float32).reshape(16, 128).T),
    )


def _host_tables(core):
    pos = np.arange(core * TOK - HALO, core * TOK + TOK, dtype=np.int64)
    pos = np.maximum(pos, 0).astype(np.float32)
    half = DH // 2
    theta = (1.0 / (10000.0 ** (np.arange(half, dtype=np.float32) / np.float32(half)))).astype(np.float32)
    freqs = pos[None, :] * theta[:, None]
    freqs = np.concatenate([freqs, freqs], axis=0)
    cos = np.cos(freqs).astype(np.float32)
    sin = np.sin(freqs).astype(np.float32)
    deny = np.zeros((40,), dtype=np.float32)
    if core == 0:
        for qt in range(8):
            for b in range(5):
                if (qt + b) * 128 + 128 <= HALO:
                    deny[qt * 5 + b] = NEG
    deny_h = np.ascontiguousarray(np.broadcast_to(deny[None, :], (128, 40)))
    ii = np.arange(128)[:, None]
    jj = np.arange(128)[None, :]
    mhm = np.where(jj >= ii + 1, 0.0, NEG).astype(np.float32)
    mdm = np.where(jj <= ii, 0.0, NEG).astype(np.float32)
    return cos, sin, deny_h, mhm, mdm


_STAGES = ("ffn1", "attn", "ffn2")
_QT_PIPE = True
_ATTN_LITE = False
_SC_EVAC = False


def _get_nc(stages):
    if stages not in _CACHE:
        _CACHE[stages] = Builder(stages).build()
    return _CACHE[stages]


def kernel(**inputs):
    nc = _get_nc(_STAGES)
    w = _host_weights(inputs)
    x = np.ascontiguousarray(np.asarray(inputs["L0_attn_x"], dtype=np.float32).reshape(8192, D))
    ident = np.eye(128, dtype=np.float32)
    in_maps = []
    for c in range(NCORES):
        lo = c * TOK - HALO
        if lo < 0:
            xs = np.concatenate([np.zeros((-lo, D), np.float32), x[0:c * TOK + TOK]], axis=0)
        else:
            xs = x[lo: c * TOK + TOK]
        cos, sin, deny, mhm, mdm = _host_tables(c)
        in_maps.append(dict(x=np.ascontiguousarray(xs), cosp=cos, sinp=sin, deny=deny,
                            mh=mhm, md=mdm, ident=ident, **w))
    res = run_bass_kernel_spmd(nc, in_maps, core_ids=list(range(NCORES)))
    _LAST["res"] = res
    y = np.concatenate([res.results[c]["y"] for c in range(NCORES)], axis=0)
    return y.reshape(1, 8192, D).astype(np.float32)


_LAST = {}
